# version 03
# baseline (speedup 1.0000x reference)
import os
import numpy as np
import ml_dtypes
import concourse.bass as bass
import concourse.mybir as mybir
from concourse.bass_utils import run_bass_kernel_spmd

F32 = mybir.dt.float32
BF16 = mybir.dt.bfloat16
AF = mybir.ActivationFunctionType
ALU = mybir.AluOpType
AX = mybir.AxisListType

B, N, D = 4, 4096, 2048
DC = D // 128
E = 16
NEXP = 18
H = 1408
HC = H // 128
HS = 2816
TT = 512
NT = 4
TCORE = 2048
HT = 256
HEADS = 16
DH = 128
EPS = 1e-5
TOPK = 6
SCALE = DH ** -0.5
NEG = -1.0e38

def b1col(e, k): return e * HC + k
def b3col(e, k): return NEXP * HC + e * HC + k
GCOL0 = 2 * NEXP * HC
BCOLS = GCOL0 + DC

W13SLAB = 128 * 2 * D
W2SLAB = 128 * H
EXPSZ = HC * W13SLAB + DC * W2SLAB
QSLAB = 128 * D
OFF_WQ = NEXP * EXPSZ
OFF_WO = OFF_WQ + DC * QSLAB
WBF_ELEMS = OFF_WO + DC * QSLAB

ENGINES = ("sync", "tensor", "vector", "scalar", "gpsimd")
ESEM = {"sync": ("sW", 16), "tensor": ("sPE", 1), "vector": ("sDVE", 1),
        "scalar": ("sSC", 1), "gpsimd": ("sG", 1)}

MK_STOP = os.environ.get("MK_STOP", "full")
MK_NT = int(os.environ.get("MK_NT", "4"))
MK_NEXPD = int(os.environ.get("MK_NEXPD", str(NEXP)))
MK_DEBUG = os.environ.get("MK_DEBUG", "0") == "1"


def local_blocks(half):
    return list(range(half, 32, 2))


def local_positions(half):
    p = []
    for pb in local_blocks(half):
        p.extend(range(128 * pb, 128 * (pb + 1)))
    return np.array(p, dtype=np.int64)


class Prog:
    def __init__(self, nc):
        self.nc = nc
        self.ops = {e: [] for e in ENGINES}
        self.sems = {}
        self.vals = {}
        for e in ENGINES:
            s, _ = ESEM[e]
            self.sems[s] = self.nc.alloc_semaphore(s)
            self.vals[s] = 0
        self.sems["sCC"] = self.nc.alloc_semaphore("sCC")
        self.vals["sCC"] = 0
        self.res = {}
        self.hwm = {e: {} for e in ENGINES}

    def _ms(self, engine, dma):
        s, step = ESEM[engine]
        isdma = dma or engine == "sync"
        self.vals[s] += 16 if isdma else step
        return (s, self.vals[s], engine, isdma)

    def op(self, engine, fn, reads=(), writes=(), dma=False, extra=(),
           coll=False, strict=False):
        waits = []
        for r in writes:
            w, rd = self.res.get(r, (None, []))
            for ms in rd:
                waits.append(("war", ms))
            if w is not None:
                waits.append(("waw", w))
        for r in reads:
            w, rd = self.res.get(r, (None, []))
            if w is not None:
                waits.append(("raw", w))
        for ms in extra:
            if ms is not None:
                waits.append(("raw", ms))
        if coll:
            s = "sCC"
            self.vals[s] += 1
            ms = (s, self.vals[s], engine, False)
        else:
            ms = self._ms(engine, dma)
        if strict:
            ms = (ms[0], ms[1], ms[2], True)
        need = {}
        for kind, w_ in waits:
            if len(w_) == 3:
                s, v, weng = w_
                wdma = True
            else:
                s, v, weng, wdma = w_
            if v is None:
                continue
            if kind in ("war", "waw") and weng == engine and not wdma:
                continue
            need[s] = max(need.get(s, 0), v)
        fwaits = []
        for s, v in need.items():
            if self.hwm[engine].get(s, 0) >= v:
                continue
            self.hwm[engine][s] = v
            fwaits.append((self.sems[s], v))
        step = 16 if (dma or (engine == "sync")) else 1
        if coll:
            step = 1
        self.ops[engine].append((fn, fwaits, self.sems[ms[0]], step))
        for r in writes:
            self.res[r] = (ms, [])
        for r in reads:
            w, rd = self.res.get(r, (None, []))
            rd.append(ms)
            self.res[r] = (w, rd)
        return ms

    def barrier(self):
        tot = {ESEM[e][0]: self.vals[ESEM[e][0]] for e in ENGINES}
        for e in ENGINES:
            extra = [(s, v, "?", True) for s, v in tot.items() if s != ESEM[e][0]]
            if e in ("sync",):
                self.op(e, lambda eng: eng.nop(), extra=extra, dma=True)
            else:
                self.op(e, lambda eng: eng.nop(), extra=extra)
        for r in self.res:
            self.res[r] = (None, [])
        for e in ENGINES:
            pass

    def build(self):
        with self.nc.Block() as block:
            for eng in ENGINES:
                ops = self.ops[eng]
                if not ops:
                    continue
                def mk(ops):
                    def run(e):
                        for fn, waits, sem, step in ops:
                            for (sh, v) in waits:
                                e.wait_ge(sh, v)
                            co = fn.__code__
                            nd = len(fn.__defaults__ or ())
                            if co.co_argcount == nd:
                                ins = fn()(e)
                            else:
                                ins = fn(e)
                            if ins is not None:
                                ins.then_inc(sem, step)
                    return run
                getattr(block, eng)(mk(ops))


def build_nc():
    nc = bass.Bass("TRN2", target_bir_lowering=False)

    dram = lambda n, s, d: nc.dram_tensor(n, s, d, kind="ExternalInput").ap()
    x_in = dram("x_in", [D, TCORE], F32)
    w1t = dram("w1t", [E, D, H], F32)
    w3t = dram("w3t", [E, D, H], F32)
    w2t = dram("w2t", [E, H, D], F32)
    ws1t = dram("ws1t", [D, HS], F32)
    ws3t = dram("ws3t", [D, HS], F32)
    ws2t = dram("ws2t", [HS, D], F32)
    wq_d = dram("wq", [D, D], F32)
    wkv_d = dram("wkv", [D, 256], F32)
    wo_d = dram("wo", [D, D], F32)
    bcols_d = dram("bcols", [128, BCOLS], F32)
    b2aug_d = dram("b2aug", [17, D], F32)
    gatep_d = dram("gatep", [128, DC * E], F32)
    gbb_d = dram("gbb", [128, E], F32)
    sel_d = dram("sel", [32, NEXP * 128], F32)
    halfrow_d = dram("halfrow", [1, TT], F32)
    identb_d = dram("identb", [128, 128], BF16)
    identf_d = dram("identf", [128, 128], F32)
    identh_d = dram("identh", [128, 128], F32)
    onescb_d = dram("onescb", [128, 1], BF16)
    onesrf_d = dram("onesrf", [1, 128], F32)
    mask_d = dram("mask", [128, 256], F32)
    cost_d = dram("costab", [128, TCORE], F32)
    sint_d = dram("sintab", [128, TCORE], F32)

    out_d = nc.dram_tensor("out", [D, TCORE], F32, kind="ExternalOutput").ap()
    if MK_DEBUG:
        dbg_xn = nc.dram_tensor("dbg_xn", [128, DC * TT], BF16,
                                kind="ExternalOutput").ap()
        dbg_gtok = nc.dram_tensor("dbg_gtok", [128, 64], F32,
                                  kind="ExternalOutput").ap()
        dbg_cwaug = nc.dram_tensor("dbg_cwaug", [32, TT], F32,
                                   kind="ExternalOutput").ap()
        dbg_y = nc.dram_tensor("dbg_y", [128, DC * TT], F32,
                               kind="ExternalOutput").ap()
        dbg_prob = nc.dram_tensor("dbg_prob", [128, 64], F32,
                                  kind="ExternalOutput").ap()
        dbg_cwtok = nc.dram_tensor("dbg_cwtok", [128, 64], F32,
                                   kind="ExternalOutput").ap()
        dbg_bias = nc.dram_tensor("dbg_bias", [128, 64], F32,
                                  kind="ExternalOutput").ap()

    idram = lambda n, s, d: nc.dram_tensor(n, s, d, kind="Internal").ap()
    WSPLIT = 9 * EXPSZ
    wbf_a = idram("wbf_a", [WSPLIT], BF16)
    wbf_b = idram("wbf_b", [WBF_ELEMS - WSPLIT], BF16)
    class _WBF:
        def __getitem__(self, sl):
            a, b = sl.start, sl.stop
            if b <= WSPLIT:
                return wbf_a[a:b]
            assert a >= WSPLIT, (a, b)
            return wbf_b[a - WSPLIT:b - WSPLIT]
    wbf = _WBF()
    xmid = idram("xmid", [NT, DC, 128, TT], F32)
    xn2scr = idram("xn2scr", [NT, DC, 128, TT], BF16)
    kcon = idram("kcon", [128, TCORE], BF16)
    vcon = idram("vcon", [TCORE, 128], BF16)
    kg = idram("kg", [256, TCORE], BF16)
    vg = idram("vg", [2 * TCORE, 128], BF16)

    ns = nc.alloc_sbuf_tensor
    nat = nc.alloc_sbuf_tensor_at
    xn = ns("xn", [128, DC * TT], BF16).ap()
    w13 = [ns(f"w13_{i}", [128, 2 * D], BF16).ap() for i in range(4)]
    w2b = [ns(f"w2b_{i}", [128, 2 * H], BF16).ap() for i in range(2)]
    xs = [ns(f"xs_{i}", [128, TT], F32).ap() for i in range(2)]
    stage = [ns(f"stage{i}", [128, 2048], F32).ap() for i in range(2)]
    bcol = ns("sb_bcol", [128, BCOLS], F32).ap()
    identb = ns("sb_identb", [128, 128], BF16).ap()
    identf = ns("sb_identf", [128, 128], F32).ap()
    identh = ns("sb_identh", [128, 128], F32).ap()
    onescb = ns("sb_onescb", [128, 1], BF16).ap()
    onesrf = ns("sb_onesrf", [1, 128], F32).ap()
    mask = ns("sb_mask", [128, 256], F32).ap()
    cost = ns("cost", [128, TCORE], F32).ap()
    sint = ns("sint", [128, TCORE], F32).ap()
    srow = ns("srow", [1, 2560], F32).ap()
    mu_r = srow[0:1, 0:512]
    m2_r = srow[0:1, 512:1024]
    rst_r = srow[0:1, 1024:1536]
    tmp_r = srow[0:1, 1536:2048]
    den_r = srow[0:1, 2048:2304]
    rr_r = srow[0:1, 2304:2560]
    pcol = ns("pcol", [128, 8], F32).ap()
    m_c = pcol[:, 0:1]
    nb_c = pcol[:, 1:2]
    thr_c = pcol[:, 2:3]

    AR = 94000
    lo, hi = nc.bump_sbuf(AR)
    cur = [lo]
    def at(name, shape, dtype):
        sz = shape[1] * mybir.dt.size(dtype)
        sz = (sz + 31) // 32 * 32
        h = nat(name, shape, dtype, offset=cur[0])
        cur[0] += sz
        assert cur[0] <= hi, (name, cur[0], hi)
        return h.ap()
    y = at("y", [128, DC * TT], F32)
    hbuf = at("hbuf", [128, HC * TT], BF16)
    sbuf_s = [at(f"ss{i}", [128, TT], F32) for i in range(2)]
    tbuf = [at(f"tb{i}", [128, TT], F32) for i in range(2)]
    xcast = [at(f"xc{i}", [128, TT], BF16) for i in range(2)]
    xsq = [at(f"xq{i}", [128, TT], BF16) for i in range(2)]
    cwb = at("cwb", [128, TT], F32)
    cwaug = at("cwaug", [32, TT], F32)
    gsc = at("gsc", [32, TT], F32)
    gtok = at("gtok", [128, 64], F32)
    gprob = at("gprob", [128, 64], F32)
    gbias = at("gbias", [128, 64], F32)
    gping = at("gping", [128, 16], F32)
    gpong = at("gpong", [128, 16], F32)
    gmsk = at("gmsk", [128, 16], F32)
    gbb = at("sb_gbb", [128, E], F32)
    cwtok = at("cwtok", [128, 64], F32)
    b2aug = at("sb_b2aug", [32, D], F32)
    gatep = at("sb_gatep", [128, DC * E], BF16)
    sel = at("sb_sel", [32, NEXP * 128], F32)
    f_end = cur[0]
    cur[0] = lo
    S0 = at("S0", [128, 4096], F32)
    S1 = at("S1", [128, 4096], F32)
    P0 = at("P0", [128, 4096], BF16)
    P1 = at("P1", [128, 4096], BF16)
    kseq = at("kseq", [128, N], BF16)
    vtok = at("vtok", [128, N], BF16)
    krot = at("krot", [128, TT], F32)
    t1s = [at(f"t1s{i}", [128, TT], F32) for i in range(2)]
    sub0 = cur[0]
    xn1h = at("xn1h", [128, DC * HT], F32)
    axc = [at(f"axc{i}", [128, HT], BF16) for i in range(2)]
    asq = [at(f"asq{i}", [128, HT], BF16) for i in range(2)]
    kvb = at("kvb", [128, TT], BF16)
    vtb = at("vtb", [128, TT], BF16)
    kv_end = cur[0]
    cur[0] = sub0
    qbuf = at("qbuf", [128, HEADS * HT], BF16)
    Ao = at("Ao", [128, HEADS * HT], BF16)
    PTb = [at(f"PTb{i}", [128, 512], BF16) for i in range(2)]
    rbc = at("rbc", [128, 512], F32)
    at_end = cur[0]
    a_end = max(kv_end, at_end)
    assert max(f_end, a_end) <= hi, (f_end, a_end, hi)

    nps = nc.alloc_psum_tensor
    PA = [nps(f"PA{i}", [128, 512], F32).ap() for i in range(2)]
    PB = [nps(f"PB{i}", [128, 512], F32).ap() for i in range(2)]
    PY = [nps(f"PY{i}", [128, 512], F32).ap() for i in range(2)]
    PT = [nps(f"PT{i}", [128, 1024], BF16).ap() for i in range(2)]
    rPA = ["PA0", "PA1"]; rPB = ["PB0", "PB1"]; rPY = ["PY0", "PY1"]
    rPT = ["PT0", "PT1"]

    P = Prog(nc)
    first_done = set()
    state = dict(stage_i=0, cast_i=0)
    cast_cycle = ["vector", "gpsimd", "scalar"]

    def slab_off(kind, a, b=0):
        if kind == "w13":
            return a * EXPSZ + b * W13SLAB
        if kind == "w2":
            return a * EXPSZ + HC * W13SLAB + b * W2SLAB
        if kind == "wq":
            return OFF_WQ + a * QSLAB
        return OFF_WO + a * QSLAB

    def f32_src(kind, a, b=0):
        if kind == "w13":
            e, k = a, b
            if e < E:
                s1 = w1t[e, :, 128 * k:128 * (k + 1)]
                s3 = w3t[e, :, 128 * k:128 * (k + 1)]
            else:
                off = (e - E) * H
                s1 = ws1t[:, off + 128 * k: off + 128 * (k + 1)]
                s3 = ws3t[:, off + 128 * k: off + 128 * (k + 1)]
            r = lambda ap: ap.rearrange("(c p) j -> p c j", p=128)
            return [(r(s1), 2048), (r(s3), 2048)]
        if kind == "w2":
            e, m = a, b
            if e < E:
                s = w2t[e, :, 128 * m:128 * (m + 1)]
            else:
                off = (e - E) * H
                s = ws2t[off:off + H, 128 * m:128 * (m + 1)]
            return [(s.rearrange("(c p) j -> p c j", p=128), H)]
        s = (wq_d if kind == "wq" else wo_d)[:, 128 * a:128 * (a + 1)]
        return [(s.rearrange("(c p) j -> p c j", p=128), 2048)]

    def load_slab(kind, a, b, dst, dst_ap, dst_off):
        key = (kind, a, b)
        off = slab_off(kind, a, b)
        if key in first_done:
            halves = f32_src(kind, a, b)
            total = sum(h[1] for h in halves)
            P.op("sync",
                 (lambda d=dst_ap, o=dst_off, t=total, woff=off:
                  (lambda sy: sy.dma_start(
                      out=d[:, o:o + t],
                      in_=wbf[woff:woff + 128 * t].rearrange("(p j) -> p j", p=128)))),
                 reads=[("wbf", key)], writes=[dst], dma=True)
            return
        first_done.add(key)
        o = dst_off
        for hi_, (src_ap, elems) in enumerate(f32_src(kind, a, b)):
            sp = state["stage_i"] % 2
            state["stage_i"] += 1
            P.op("sync",
                 (lambda s=src_ap, sp=sp, el=elems:
                  (lambda sy: sy.dma_start(out=stage[sp][:, 0:el], in_=s))),
                 writes=[f"stage{sp}"], dma=True)
            eng = cast_cycle[state["cast_i"] % 3]
            state["cast_i"] += 1
            if eng == "scalar":
                castfn = (lambda d=dst_ap, oo=o, sp=sp, el=elems:
                          (lambda e: e.copy(out=d[:, oo:oo + el],
                                            in_=stage[sp][:, 0:el])))
            else:
                castfn = (lambda d=dst_ap, oo=o, sp=sp, el=elems:
                          (lambda e: e.tensor_copy(out=d[:, oo:oo + el],
                                                   in_=stage[sp][:, 0:el])))
            P.op(eng, castfn, reads=[f"stage{sp}"], writes=[dst])
            tot = sum(h2[1] for h2 in f32_src(kind, a, b))
            P.op("sync",
                 (lambda d=dst_ap, oo=o, el=elems, tot=tot,
                         o0=dst_off, woff=off:
                  (lambda sy: sy.dma_start(
                      out=wbf[woff:woff + 128 * tot].rearrange(
                          "(p j) -> p j", p=128)[:, oo - o0:oo - o0 + el],
                      in_=d[:, oo:oo + el]))),
                 reads=[dst], writes=[("wbf", key)], dma=True)
            o += elems

    def ffn_phase(sub, src_ext, dst_out):
        def xsrc(t, c):
            if src_ext:
                return x_in[128 * c:128 * (c + 1), TT * t:TT * (t + 1)]
            return xmid[t, c]
        def xdst(t, c):
            if dst_out:
                return out_d[128 * c:128 * (c + 1), TT * t:TT * (t + 1)]
            return xmid[t, c]
        rsrc = "x_ext" if src_ext else "xmid"

        for t in range(MK_NT):
            for c in range(DC):
                ci = c % 2
                P.op("scalar",
                     (lambda t=t, c=c, ci=ci:
                      (lambda e: e.dma_start(out=xs[ci][:], in_=xsrc(t, c)))),
                     reads=[rsrc], writes=[f"xs{ci}"], dma=True)
                P.op("gpsimd",
                     (lambda ci=ci: (lambda e: e.tensor_copy(
                         out=xcast[ci][:], in_=xs[ci][:]))),
                     reads=[f"xs{ci}"], writes=[f"xc{ci}"])
                P.op("scalar",
                     (lambda ci=ci: (lambda e: e.activation(
                         out=xsq[ci][:], in_=xs[ci][:], func=AF.Square))),
                     reads=[f"xs{ci}"], writes=[f"xq{ci}"])
                P.op("tensor",
                     (lambda ci=ci, c=c: (lambda e: e.matmul(
                         out=PY[0][0:1, :], lhsT=onescb, rhs=xcast[ci][:],
                         start=(c == 0), stop=(c == DC - 1)))),
                     reads=[f"xc{ci}", "c_onescb"],
                     writes=["PY0"] if c == 0 else [], )
                if c > 0:
                    P.res["PY0"] = (((("sPE", P.vals["sPE"], "tensor", False)), []))
                P.op("tensor",
                     (lambda ci=ci, c=c: (lambda e: e.matmul(
                         out=PY[1][0:1, :], lhsT=onescb, rhs=xsq[ci][:],
                         start=(c == 0), stop=(c == DC - 1)))),
                     reads=[f"xq{ci}"], writes=["PY1"] if c == 0 else [])
                if c > 0:
                    P.res["PY1"] = (((("sPE", P.vals["sPE"], "tensor", False)), []))
            P.op("vector", lambda e: e.tensor_scalar_mul(
                out=mu_r, in0=PY[0][0:1, :], scalar1=1.0 / D),
                reads=["PY0"], writes=["r_mu"])
            P.op("vector", lambda e: e.tensor_scalar_mul(
                out=m2_r, in0=PY[1][0:1, :], scalar1=1.0 / D),
                reads=["PY1"], writes=["r_m2"])
            P.op("vector", lambda e: e.tensor_mul(
                out=tmp_r, in0=mu_r, in1=mu_r), reads=["r_mu"], writes=["r_tmp"])
            P.op("vector", lambda e: e.scalar_tensor_tensor(
                out=m2_r, in0=m2_r, scalar=EPS, in1=tmp_r,
                op0=ALU.add, op1=ALU.subtract),
                reads=["r_tmp"], writes=["r_m2"])
            P.op("scalar", lambda e: e.activation(
                out=tmp_r, in_=m2_r, func=AF.Ln),
                reads=["r_m2"], writes=["r_tmp"])
            P.op("scalar", lambda e: e.activation(
                out=rst_r, in_=tmp_r, func=AF.Exp, scale=-0.5),
                reads=["r_tmp"], writes=["r_rst"])
            P.op("tensor", lambda e: e.matmul(
                out=PB[0][:], lhsT=onesrf, rhs=mu_r, start=True, stop=True),
                reads=["r_mu", "c_onesrf"], writes=["PB0"])
            P.op("tensor", lambda e: e.matmul(
                out=PB[1][:], lhsT=onesrf, rhs=rst_r, start=True, stop=True),
                reads=["r_rst"], writes=["PB1"])
            for c in range(DC):
                ci = c % 2
                P.op("scalar",
                     (lambda t=t, c=c, ci=ci:
                      (lambda e: e.dma_start(out=xs[ci][:], in_=xsrc(t, c)))),
                     reads=[rsrc], writes=[f"xs{ci}"], dma=True)
                P.op("vector",
                     (lambda ci=ci: (lambda e: e.tensor_sub(
                         out=tbuf[ci][:], in0=xs[ci][:], in1=PB[0][:]))),
                     reads=[f"xs{ci}", "PB0"], writes=[f"tb{ci}"])
                P.op("vector",
                     (lambda ci=ci, c=c: (lambda e: e.scalar_tensor_tensor(
                         out=xn[:, TT * c:TT * (c + 1)], in0=tbuf[ci][:],
                         scalar=1.0, in1=PB[1][:], op0=ALU.mult, op1=ALU.mult))),
                     reads=[f"tb{ci}", "PB1"], writes=["xn"] if c == 0 else [])
            ms_norm = ("sDVE", P.vals["sDVE"], "vector", False)
            P.res["xn"] = (ms_norm, [])
            for c in range(DC):
                P.op("tensor",
                     (lambda c=c: (lambda e: e.matmul(
                         out=PA[0][0:16, :], lhsT=gatep[:, E * c:E * (c + 1)],
                         rhs=xn[:, TT * c:TT * (c + 1)],
                         start=(c == 0), stop=(c == DC - 1)))),
                     reads=["xn", "c_gatep"], writes=["PA0"] if c == 0 else [])
            P.res["PA0"] = (((("sPE", P.vals["sPE"], "tensor", False)), []))
            P.op("vector", lambda e: e.tensor_copy(
                out=gsc[0:16, :], in_=PA[0][0:16, :]),
                reads=["PA0"], writes=["gsc"])
            for j in range(4):
                P.op("tensor",
                     (lambda j=j: (lambda e: e.matmul(
                         out=PA[1][:, 16 * j:16 * (j + 1)],
                         lhsT=gsc[0:16, 128 * j:128 * (j + 1)],
                         rhs=identf[0:16, 0:16], start=True, stop=True))),
                     reads=["gsc", "c_identf"],
                     writes=["PA1"] if j == 0 else [])
            P.res["PA1"] = (((("sPE", P.vals["sPE"], "tensor", False)), []))
            P.op("vector", lambda e: e.tensor_copy(out=gtok[:], in_=PA[1][:, 0:64]),
                 reads=["PA1"], writes=["gtok"])
            for j in range(4):
                gt = gtok[:, 16 * j:16 * (j + 1)]
                pj = gprob[:, 16 * j:16 * (j + 1)]
                bj = gbias[:, 16 * j:16 * (j + 1)]
                cj = cwtok[:, 16 * j:16 * (j + 1)]
                P.op("vector", (lambda gt=gt: (lambda e: e.reduce_max(
                    out=m_c, in_=gt, axis=AX.X))), reads=["gtok"], writes=["mc"], strict=True)
                P.op("vector", lambda e: e.tensor_scalar_mul(
                    out=m_c, in0=m_c, scalar1=-1.0), reads=["mc"], writes=["mc"])
                P.op("scalar",
                     (lambda gt=gt, pj=pj: (lambda e: e.activation(
                         out=pj, in_=gt, func=AF.Exp, bias=m_c))),
                     reads=["gtok", "mc"], writes=["gprob"])
                P.op("vector", (lambda pj=pj: (lambda e: e.reduce_sum(
                    out=m_c, in_=pj, axis=AX.X))), reads=["gprob"], writes=["mc"], strict=True)
                P.op("vector", lambda e: e.reciprocal(out=nb_c, in_=m_c),
                     reads=["mc"], writes=["nbc"], strict=True)
                P.op("vector", (lambda pj=pj: (lambda e: e.tensor_scalar_mul(
                    out=pj, in0=pj, scalar1=nb_c))),
                    reads=["nbc", "gprob"], writes=["gprob"])
                P.op("vector", (lambda pj=pj, bj=bj: (lambda e: e.tensor_add(
                    out=bj, in0=pj, in1=gbb))),
                    reads=["gprob", "c_gbb"], writes=["gbias"])
                P.op("vector", (lambda bj=bj: (lambda e: e.tensor_copy(
                    out=gping[:], in_=bj))), reads=["gbias"], writes=["gping"])
                cur_b, cur_r, nxt_b, nxt_r = gping, "gping", gpong, "gpong"
                for it in range(TOPK - 1):
                    P.op("vector", (lambda cb=cur_b: (lambda e: e.reduce_max(
                        out=thr_c, in_=cb, axis=AX.X))),
                        reads=[cur_r], writes=["thrc"], strict=True)
                    P.op("vector", (lambda cb=cur_b: (lambda e: e.tensor_scalar(
                        out=gmsk[:], in0=cb, scalar1=thr_c, scalar2=NEG,
                        op0=ALU.is_ge, op1=ALU.mult))),
                        reads=[cur_r, "thrc"], writes=["gmsk"])
                    P.op("vector", (lambda cb=cur_b, nb=nxt_b:
                                    (lambda e: e.tensor_add(
                                        out=nb, in0=cb, in1=gmsk[:]))),
                         reads=[cur_r, "gmsk"], writes=[nxt_r])
                    cur_b, cur_r, nxt_b, nxt_r = nxt_b, nxt_r, cur_b, cur_r
                P.op("vector", (lambda cb=cur_b: (lambda e: e.reduce_max(
                    out=thr_c, in_=cb, axis=AX.X))), reads=[cur_r],
                    writes=["thrc"], strict=True)
                P.op("vector", (lambda bj=bj: (lambda e: e.tensor_single_scalar(
                    out=gmsk[:], in_=bj, scalar=thr_c, op=ALU.is_ge))),
                    reads=["gbias", "thrc"], writes=["gmsk"])
                P.op("vector", (lambda pj=pj, cj=cj: (lambda e: e.tensor_mul(
                    out=cj, in0=pj, in1=gmsk[:]))),
                    reads=["gprob", "gmsk"], writes=["cwtok"] if j == 3 else [])
            P.res["cwtok"] = (((("sDVE", P.vals["sDVE"], "vector", False)), []))
            for j in range(4):
                P.op("tensor",
                     (lambda j=j: (lambda e: e.matmul(
                         out=PB[0][0:16, 128 * j:128 * (j + 1)],
                         lhsT=cwtok[:, 16 * j:16 * (j + 1)],
                         rhs=identh, start=True, stop=True))),
                     reads=["cwtok", "c_identh"],
                     writes=["PB0"] if j == 0 else [])
            P.res["PB0"] = (((("sPE", P.vals["sPE"], "tensor", False)), []))
            P.op("vector", lambda e: e.tensor_copy(
                out=cwaug[0:16, :], in_=PB[0][0:16, :]),
                reads=["PB0"], writes=["cwaug"])
            for m in range(DC):
                mp = m % 2
                P.op("tensor",
                     (lambda m=m, mp=mp: (lambda e: e.matmul(
                         out=PY[mp][:], lhsT=b2aug[0:17, 128 * m:128 * (m + 1)],
                         rhs=cwaug[0:17, :], start=True, stop=True))),
                     reads=["cwaug", "c_b2aug"], writes=[rPY[mp]])
                P.op("vector",
                     (lambda m=m, mp=mp: (lambda e: e.tensor_copy(
                         out=y[:, TT * m:TT * (m + 1)], in_=PY[mp][:]))),
                     reads=[rPY[mp]], writes=["y"] if m == 0 else [])
            P.res["y"] = (((("sDVE", P.vals["sDVE"], "vector", False)), []))

            for ex in range(MK_NEXPD):
                gidx0 = ((sub - 1) * NT + t) * NEXP + ex
                kpar0 = (gidx0 * HC) % 2
                ub = (kpar0 + 1) % 2
                P.op("tensor",
                     (lambda ex=ex, ub=ub: (lambda e: e.matmul(
                         out=PA[ub][:], lhsT=sel[:, 128 * ex:128 * (ex + 1)],
                         rhs=cwaug[0:32, :], start=True, stop=True))),
                     reads=["cwaug", "c_sel"], writes=[rPA[ub]])
                P.op("vector",
                     (lambda ub=ub: (lambda e: e.tensor_copy(
                         out=cwb[:], in_=PA[ub][:]))),
                     reads=[rPA[ub]], writes=["cwb"])
                for k in range(HC):
                    gk = gidx0 * HC + k
                    par = gk % 2
                    sl = gk % 4
                    load_slab("w13", ex, k, f"w13b{sl}", w13[sl], 0)
                    def agrp(e, par=par, sl=sl):
                        for c in range(DC):
                            mm = e.matmul(out=PA[par][:],
                                          lhsT=w13[sl][:, 128 * c:128 * (c + 1)],
                                          rhs=xn[:, TT * c:TT * (c + 1)],
                                          start=(c == 0), stop=(c == DC - 1))
                        return mm
                    P.op("tensor", agrp, reads=[f"w13b{sl}", "xn"],
                         writes=[rPA[par]])
                    def bgrp(e, par=par, sl=sl):
                        for c in range(DC):
                            mm = e.matmul(out=PB[par][:],
                                          lhsT=w13[sl][:, D + 128 * c:D + 128 * (c + 1)],
                                          rhs=xn[:, TT * c:TT * (c + 1)],
                                          start=(c == 0), stop=(c == DC - 1))
                        return mm
                    P.op("tensor", bgrp, reads=[f"w13b{sl}", "xn"],
                         writes=[rPB[par]])
                    col = b1col(ex, k)
                    P.op("scalar",
                         (lambda par=par, col=col: (lambda e: e.activation(
                             out=sbuf_s[par][:], in_=PA[par][:], func=AF.Silu,
                             bias=bcol[:, col:col + 1]))),
                         reads=[rPA[par], "c_bcol"], writes=[f"ss{par}"])
                    col = b3col(ex, k)
                    P.op("vector",
                         (lambda par=par, col=col: (lambda e:
                          e.scalar_tensor_tensor(
                              out=tbuf[par][:], in0=PB[par][:],
                              scalar=bcol[:, col:col + 1], in1=sbuf_s[par][:],
                              op0=ALU.add, op1=ALU.mult))),
                         reads=[rPB[par], f"ss{par}", "c_bcol"],
                         writes=[f"tb{par}"])
                    P.op("gpsimd",
                         (lambda par=par, k=k: (lambda e: e.tensor_mul(
                             out=hbuf[:, TT * k:TT * (k + 1)],
                             in0=tbuf[par][:], in1=cwb[:]))),
                         reads=[f"tb{par}", "cwb"],
                         writes=["hbuf"] if k == 0 else [])
                P.res["hbuf"] = (((("sG", P.vals["sG"], "gpsimd", False)), []))
                for mp in range(DC // 2):
                    gp_ = gidx0 * (DC // 2) + mp
                    w2i = gp_ % 2
                    key = ("w2", ex, 2 * mp)
                    if key in first_done:
                        for m2 in range(2):
                            off = slab_off("w2", ex, 2 * mp + m2)
                            P.op("sync",
                                 (lambda bb=w2b[w2i], woff=off, m2=m2:
                                  (lambda sy: sy.dma_start(
                                      out=bb[:, m2 * H:(m2 + 1) * H],
                                      in_=wbf[woff:woff + 128 * H].rearrange(
                                          "(p j) -> p j", p=128)))),
                                 reads=[("wbf", ("w2", ex, 2 * mp + m2))],
                                 writes=[f"w2b{w2i}"], dma=True)
                    else:
                        first_done.add(key)
                        first_done.discard(key)
                        for m2 in range(2):
                            load_slab("w2", ex, 2 * mp + m2,
                                      f"w2b{w2i}", w2b[w2i], m2 * H)
                        first_done.add(key)
                    for m2 in range(2):
                        m = 2 * mp + m2
                        def y2g(e, m=m, w2i=w2i):
                            off = (m % 2) * H
                            for k in range(HC):
                                mm = e.matmul(
                                    out=PY[m % 2][:],
                                    lhsT=w2b[w2i][:, off + 128 * k:
                                                  off + 128 * (k + 1)],
                                    rhs=hbuf[:, TT * k:TT * (k + 1)],
                                    start=(k == 0), stop=(k == HC - 1))
                            return mm
                        P.op("tensor", y2g, reads=[f"w2b{w2i}", "hbuf"],
                             writes=[rPY[m % 2]])
                        P.op("vector",
                             (lambda m=m: (lambda e: e.tensor_add(
                                 out=y[:, TT * m:TT * (m + 1)],
                                 in0=y[:, TT * m:TT * (m + 1)],
                                 in1=PY[m % 2][:]))),
                             reads=[rPY[m % 2], "y"], writes=["y"])

            if MK_DEBUG and sub == 1 and t == 0:
                P.op("gpsimd", lambda e: e.dma_start(out=dbg_xn, in_=xn[:]),
                     reads=["xn"], writes=["dbg"], dma=True)
                P.op("gpsimd", lambda e: e.dma_start(out=dbg_gtok, in_=gtok[:]),
                     reads=["gtok"], writes=["dbg"], dma=True)
                P.op("gpsimd", lambda e: e.dma_start(out=dbg_prob, in_=gprob[:]),
                     reads=["gprob"], writes=["dbg"], dma=True)
                P.op("gpsimd", lambda e: e.dma_start(out=dbg_cwaug, in_=cwaug[:]),
                     reads=["cwaug"], writes=["dbg"], dma=True)
                P.op("gpsimd", lambda e: e.dma_start(out=dbg_cwtok, in_=cwtok[:]),
                     reads=["cwtok"], writes=["dbg"], dma=True)
                P.op("gpsimd", lambda e: e.dma_start(out=dbg_bias, in_=gbias[:]),
                     reads=["gbias"], writes=["dbg"], dma=True)
                P.op("gpsimd", lambda e: e.dma_start(out=dbg_y, in_=y[:]),
                     reads=["y"], writes=["dbg"], dma=True)
            for m in range(DC):
                ci = m % 2
                P.op("scalar",
                     (lambda t=t, m=m, ci=ci:
                      (lambda e: e.dma_start(out=xs[ci][:], in_=xsrc(t, m)))),
                     reads=[rsrc], writes=[f"xs{ci}"], dma=True)
                P.op("vector",
                     (lambda m=m, ci=ci: (lambda e: e.tensor_add(
                         out=xs[ci][:], in0=xs[ci][:],
                         in1=y[:, TT * m:TT * (m + 1)]))),
                     reads=["y"], writes=[f"xs{ci}"])
                P.op("scalar",
                     (lambda t=t, m=m, ci=ci:
                      (lambda e: e.dma_start(out=xdst(t, m), in_=xs[ci][:]))),
                     reads=[f"xs{ci}"],
                     writes=["out" if dst_out else "xmid"], dma=True)

    def kv_pass():
        for mslab in range(2):
            sp = state["stage_i"] % 2
            state["stage_i"] += 1
            src = wkv_d[:, 128 * mslab:128 * (mslab + 1)].rearrange(
                "(c p) j -> p c j", p=128)
            P.op("sync",
                 (lambda s=src, sp=sp: (lambda sy: sy.dma_start(
                     out=stage[sp][:], in_=s))),
                 writes=[f"stage{sp}"], dma=True)
            P.op("vector",
                 (lambda sp=sp, mslab=mslab: (lambda e: e.tensor_copy(
                     out=w13[0][:, 2048 * mslab:2048 * (mslab + 1)],
                     in_=stage[sp][:]))),
                 reads=[f"stage{sp}"], writes=["w13b0"])

        for t in range(NT):
            for hh in range(2):
                c0 = HT * hh
                for c in range(DC):
                    ci = c % 2
                    P.op("scalar",
                         (lambda t=t, c=c, ci=ci, c0=c0:
                          (lambda e: e.dma_start(
                              out=xs[ci][:, 0:HT],
                              in_=xmid[t, c][:, c0:c0 + HT]))),
                         reads=["xmid"], writes=[f"xs{ci}"], dma=True)
                    P.op("gpsimd",
                         (lambda ci=ci: (lambda e: e.tensor_copy(
                             out=axc[ci][:], in_=xs[ci][:, 0:HT]))),
                         reads=[f"xs{ci}"], writes=[f"axc{ci}"])
                    P.op("scalar",
                         (lambda ci=ci: (lambda e: e.activation(
                             out=asq[ci][:], in_=xs[ci][:, 0:HT],
                             func=AF.Square))),
                         reads=[f"xs{ci}"], writes=[f"asq{ci}"])
                    P.op("tensor",
                         (lambda ci=ci, c=c: (lambda e: e.matmul(
                             out=PY[0][0:1, 0:HT], lhsT=onescb, rhs=axc[ci][:],
                             start=(c == 0), stop=(c == DC - 1)))),
                         reads=[f"axc{ci}", "c_onescb"],
                         writes=["PY0"] if c == 0 else [])
                    if c > 0:
                        P.res["PY0"] = (((("sPE", P.vals["sPE"], "tensor", False)), []))
                    P.op("tensor",
                         (lambda ci=ci, c=c: (lambda e: e.matmul(
                             out=PY[1][0:1, 0:HT], lhsT=onescb, rhs=asq[ci][:],
                             start=(c == 0), stop=(c == DC - 1)))),
                         reads=[f"asq{ci}"], writes=["PY1"] if c == 0 else [])
                    if c > 0:
                        P.res["PY1"] = (((("sPE", P.vals["sPE"], "tensor", False)), []))
                def statmath():
                    P.op("vector", lambda e: e.tensor_scalar_mul(
                        out=mu_r[0:1, 0:HT], in0=PY[0][0:1, 0:HT],
                        scalar1=1.0 / D), reads=["PY0"], writes=["r_mu"])
                    P.op("vector", lambda e: e.tensor_scalar_mul(
                        out=m2_r[0:1, 0:HT], in0=PY[1][0:1, 0:HT],
                        scalar1=1.0 / D), reads=["PY1"], writes=["r_m2"])
                    P.op("vector", lambda e: e.tensor_mul(
                        out=tmp_r[0:1, 0:HT], in0=mu_r[0:1, 0:HT],
                        in1=mu_r[0:1, 0:HT]), reads=["r_mu"], writes=["r_tmp"])
                    P.op("vector", lambda e: e.scalar_tensor_tensor(
                        out=m2_r[0:1, 0:HT], in0=m2_r[0:1, 0:HT], scalar=EPS,
                        in1=tmp_r[0:1, 0:HT], op0=ALU.add, op1=ALU.subtract),
                        reads=["r_tmp"], writes=["r_m2"])
                    P.op("scalar", lambda e: e.activation(
                        out=tmp_r[0:1, 0:HT], in_=m2_r[0:1, 0:HT], func=AF.Ln),
                        reads=["r_m2"], writes=["r_tmp"])
                    P.op("scalar", lambda e: e.activation(
                        out=rst_r[0:1, 0:HT], in_=tmp_r[0:1, 0:HT],
                        func=AF.Exp, scale=-0.5),
                        reads=["r_tmp"], writes=["r_rst"])
                    P.op("tensor", lambda e: e.matmul(
                        out=PB[0][:, 0:HT], lhsT=onesrf, rhs=mu_r[0:1, 0:HT],
                        start=True, stop=True),
                        reads=["r_mu", "c_onesrf"], writes=["PB0"])
                    P.op("tensor", lambda e: e.matmul(
                        out=PB[1][:, 0:HT], lhsT=onesrf, rhs=rst_r[0:1, 0:HT],
                        start=True, stop=True), reads=["r_rst"], writes=["PB1"])
                statmath()
                for c in range(DC):
                    ci = c % 2
                    P.op("scalar",
                         (lambda t=t, c=c, ci=ci, c0=c0:
                          (lambda e: e.dma_start(
                              out=xs[ci][:, 0:HT],
                              in_=xmid[t, c][:, c0:c0 + HT]))),
                         reads=["xmid"], writes=[f"xs{ci}"], dma=True)
                    P.op("vector",
                         (lambda ci=ci: (lambda e: e.tensor_sub(
                             out=t1s[ci][:, 0:HT], in0=xs[ci][:, 0:HT],
                             in1=PB[0][:, 0:HT]))),
                         reads=[f"xs{ci}", "PB0"], writes=[f"t1s{ci}"])
                    P.op("vector",
                         (lambda ci=ci, c=c: (lambda e: e.scalar_tensor_tensor(
                             out=xn1h[:, HT * c:HT * (c + 1)],
                             in0=t1s[ci][:, 0:HT], scalar=1.0,
                             in1=PB[1][:, 0:HT], op0=ALU.mult, op1=ALU.mult))),
                         reads=[f"t1s{ci}", "PB1"],
                         writes=["xn1h"] if c == 0 else [])
                P.res["xn1h"] = (((("sDVE", P.vals["sDVE"], "vector", False)), []))
                for c in range(DC):
                    ci = c % 2
                    P.op("gpsimd",
                         (lambda ci=ci, c=c: (lambda e: e.tensor_copy(
                             out=axc[ci][:], in_=xn1h[:, HT * c:HT * (c + 1)]))),
                         reads=["xn1h"], writes=[f"axc{ci}"])
                    P.op("scalar",
                         (lambda ci=ci, c=c: (lambda e: e.activation(
                             out=asq[ci][:], in_=xn1h[:, HT * c:HT * (c + 1)],
                             func=AF.Square))),
                         reads=["xn1h"], writes=[f"asq{ci}"])
                    P.op("tensor",
                         (lambda ci=ci, c=c: (lambda e: e.matmul(
                             out=PY[0][0:1, 0:HT], lhsT=onescb, rhs=axc[ci][:],
                             start=(c == 0), stop=(c == DC - 1)))),
                         reads=[f"axc{ci}", "c_onescb"],
                         writes=["PY0"] if c == 0 else [])
                    if c > 0:
                        P.res["PY0"] = (((("sPE", P.vals["sPE"], "tensor", False)), []))
                    P.op("tensor",
                         (lambda ci=ci, c=c: (lambda e: e.matmul(
                             out=PY[1][0:1, 0:HT], lhsT=onescb, rhs=asq[ci][:],
                             start=(c == 0), stop=(c == DC - 1)))),
                         reads=[f"asq{ci}"], writes=["PY1"] if c == 0 else [])
                    if c > 0:
                        P.res["PY1"] = (((("sPE", P.vals["sPE"], "tensor", False)), []))
                statmath()
                for c in range(DC):
                    ci = c % 2
                    P.op("vector",
                         (lambda ci=ci, c=c: (lambda e: e.tensor_sub(
                             out=t1s[ci][:, 0:HT],
                             in0=xn1h[:, HT * c:HT * (c + 1)],
                             in1=PB[0][:, 0:HT]))),
                         reads=["xn1h", "PB0"], writes=[f"t1s{ci}"])
                    P.op("vector",
                         (lambda ci=ci, c=c, c0=c0: (lambda e:
                          e.scalar_tensor_tensor(
                              out=xn[:, TT * c + c0:TT * c + c0 + HT],
                              in0=t1s[ci][:, 0:HT],
                              scalar=bcol[:, GCOL0 + c:GCOL0 + c + 1],
                              in1=PB[1][:, 0:HT], op0=ALU.mult, op1=ALU.mult))),
                         reads=[f"t1s{ci}", "PB1", "c_bcol"],
                         writes=["xn"] if c == 0 else [])
                P.res["xn"] = (((("sDVE", P.vals["sDVE"], "vector", False)), []))
                P.op("scalar",
                     (lambda t=t, c0=c0: (lambda e: e.dma_start(
                         out=xn2scr[t][:, :, c0:c0 + HT].rearrange(
                             "c p u -> p c u"),
                         in_=xn[:].rearrange("p (c u) -> p c u", c=DC)[
                             :, :, c0:c0 + HT]))),
                     reads=["xn"], writes=["xn2scr"], dma=True)
                def kvmm(e, c0=c0):
                    for c in range(DC):
                        e.matmul(out=PA[0][:, 0:HT],
                                 lhsT=w13[0][:, 128 * c:128 * (c + 1)],
                                 rhs=xn[:, TT * c + c0:TT * c + c0 + HT],
                                 start=(c == 0), stop=(c == DC - 1))
                    for c in range(DC):
                        mm = e.matmul(out=PA[0][:, HT:2 * HT],
                                      lhsT=w13[0][:, D + 128 * c:
                                                  D + 128 * (c + 1)],
                                      rhs=xn[:, TT * c + c0:TT * c + c0 + HT],
                                      start=(c == 0), stop=(c == DC - 1))
                    return mm
                P.op("tensor", kvmm, reads=["w13b0", "xn"], writes=["PA0"])
                P.op("vector", lambda e: e.tensor_copy(
                    out=krot[0:64, 0:HT], in_=PA[0][64:128, 0:HT]),
                    reads=["PA0"], writes=["krot"])
                P.op("vector", lambda e: e.tensor_copy(
                    out=krot[64:128, 0:HT], in_=PA[0][0:64, 0:HT]),
                    reads=["PA0"], writes=["krot"])
                cs = cost[:, TT * t + c0:TT * t + c0 + HT]
                sn = sint[:, TT * t + c0:TT * t + c0 + HT]
                P.op("vector", (lambda cs=cs: (lambda e: e.tensor_mul(
                    out=t1s[0][:, 0:HT], in0=PA[0][:, 0:HT], in1=cs))),
                    reads=["PA0", "c_tab"], writes=["t1s0"])
                P.op("vector", (lambda sn=sn: (lambda e: e.tensor_mul(
                    out=t1s[1][:, 0:HT], in0=krot[:, 0:HT], in1=sn))),
                    reads=["krot", "c_tab"], writes=["t1s1"])
                P.op("vector", lambda e: e.tensor_add(
                    out=kvb[:, 0:HT], in0=t1s[0][:, 0:HT], in1=t1s[1][:, 0:HT]),
                    reads=["t1s0", "t1s1"], writes=["kvb"])
                P.op("scalar",
                     (lambda t=t, c0=c0: (lambda e: e.dma_start(
                         out=kcon[:, TT * t + c0:TT * t + c0 + HT],
                         in_=kvb[:, 0:HT]))),
                     reads=["kvb"], writes=["kcon"], dma=True)
                P.op("vector", lambda e: e.tensor_copy(
                    out=vtb[:, 0:HT], in_=PA[0][:, HT:2 * HT]),
                    reads=["PA0"], writes=["vtb"])
                for j in range(2):
                    P.op("tensor",
                         (lambda j=j: (lambda e: e.transpose(
                             out=PT[0][:, 128 * j:128 * (j + 1)],
                             in_=vtb[:, 128 * j:128 * (j + 1)],
                             identity=identb))),
                         reads=["vtb", "c_identb"],
                         writes=["PT0"] if j == 0 else [])
                P.res["PT0"] = (((("sPE", P.vals["sPE"], "tensor", False)), []))
                P.op("vector", lambda e: e.tensor_copy(
                    out=kvb[:, 0:HT], in_=PT[0][:, 0:HT]),
                    reads=["PT0"], writes=["kvb"])
                P.op("scalar",
                     (lambda t=t, c0=c0: (lambda e: e.dma_start(
                         out=vcon[TT * t + c0:TT * t + c0 + HT, :].rearrange(
                             "(j p) i -> p j i", p=128),
                         in_=kvb[:, 0:HT].rearrange("p (j i) -> p j i", j=2)))),
                     reads=["kvb"], writes=["vcon"], dma=True)

        P.op("gpsimd", lambda e: e.collective_compute(
            "AllGather", ALU.bypass,
            replica_groups=[[0, 1], [2, 3], [4, 5], [6, 7]],
            ins=[kcon], outs=[kg]),
            reads=["kcon"], writes=["kg"], coll=True)
        P.op("gpsimd", lambda e: e.collective_compute(
            "AllGather", ALU.bypass,
            replica_groups=[[0, 1], [2, 3], [4, 5], [6, 7]],
            ins=[vcon], outs=[vg]),
            reads=["vcon"], writes=["vg"], coll=True)
        for par2 in range(2):
            P.op("scalar",
                 (lambda par2=par2: (lambda e: e.dma_start(
                     out=kseq[:].rearrange("p (g two j) -> p g two j",
                                           two=2, j=128)[:, :, par2, :],
                     in_=kg[128 * par2:128 * (par2 + 1), :].rearrange(
                         "p (g j) -> p g j", j=128)))),
                 reads=["kg"], writes=["kseq"], dma=True)
            P.op("scalar",
                 (lambda par2=par2: (lambda e: e.dma_start(
                     out=vtok[:].rearrange("p (g two j) -> p g two j",
                                           two=2, j=128)[:, :, par2, :],
                     in_=vg[2048 * par2:2048 * (par2 + 1), :].rearrange(
                         "(g p) j -> p g j", p=128)))),
                 reads=["vg"], writes=["vtok"], dma=True)

    def attn_pass():
        for t in range(NT):
            P.op("scalar",
                 (lambda t=t: (lambda e: e.dma_start(
                     out=xn[:], in_=xn2scr[t].rearrange("c p u -> p c u")))),
                 reads=["xn2scr"], writes=["xn"], dma=True)
            for hh in range(2):
                c0 = HT * hh
                for m in range(HEADS):
                    par = m % 2
                    sl = m % 4
                    load_slab("wq", m, 0, f"w13b{sl}", w13[sl], 0)
                    def qmm(e, par=par, sl=sl, c0=c0):
                        for c in range(DC):
                            mm = e.matmul(
                                out=PA[par][:, 0:HT],
                                lhsT=w13[sl][:, 128 * c:128 * (c + 1)],
                                rhs=xn[:, TT * c + c0:TT * c + c0 + HT],
                                start=(c == 0), stop=(c == DC - 1))
                        return mm
                    P.op("tensor", qmm, reads=[f"w13b{sl}", "xn"],
                         writes=[rPA[par]])
                    P.op("vector", (lambda par=par: (lambda e: e.tensor_copy(
                        out=krot[0:64, 0:HT], in_=PA[par][64:128, 0:HT]))),
                        reads=[rPA[par]], writes=["krot"])
                    P.op("vector", (lambda par=par: (lambda e: e.tensor_copy(
                        out=krot[64:128, 0:HT], in_=PA[par][0:64, 0:HT]))),
                        reads=[rPA[par]], writes=["krot"])
                    cs = cost[:, TT * t + c0:TT * t + c0 + HT]
                    sn = sint[:, TT * t + c0:TT * t + c0 + HT]
                    P.op("vector", (lambda par=par, cs=cs: (lambda e:
                        e.tensor_mul(out=t1s[0][:, 0:HT],
                                     in0=PA[par][:, 0:HT], in1=cs))),
                        reads=[rPA[par], "c_tab"], writes=["t1s0"])
                    P.op("vector", (lambda sn=sn: (lambda e: e.tensor_mul(
                        out=t1s[1][:, 0:HT], in0=krot[:, 0:HT], in1=sn))),
                        reads=["krot", "c_tab"], writes=["t1s1"])
                    P.op("vector", (lambda m=m: (lambda e: e.tensor_add(
                        out=qbuf[:, HT * m:HT * (m + 1)], in0=t1s[0][:, 0:HT],
                        in1=t1s[1][:, 0:HT]))),
                        reads=["t1s0", "t1s1"],
                        writes=["qbuf"] if m == 0 else [])
                P.res["qbuf"] = (((("sDVE", P.vals["sDVE"], "vector", False)), []))
                for qb in range(2):
                    lq = 4 * t + 2 * hh + qb
                    L = 256 * (lq + 1)
                    NJ = L // 128
                    KC = (L + 511) // 512
                    qoff = 128 * qb
                    for hg in range(8):
                        hA, hB = 2 * hg, 2 * hg + 1
                        gi = ((t * 2 + hh) * 2 + qb) * 8 + hg
                        opar = gi % 2
                        Sb = [S0, S1]
                        Pb = [P0, P1]
                        Sr = ["S0b", "S1b"]
                        Pr = ["P0b", "P1b"]
                        for hs, h in enumerate((hA, hB)):
                            for kc in range(KC):
                                nn = min(512, L - 512 * kc)
                                def smm(e, h=h, kc=kc, nn=nn, hs=hs, qoff=qoff):
                                    return e.matmul(
                                        out=PA[hs][:, 0:nn],
                                        lhsT=qbuf[:, HT * h + qoff:
                                                  HT * h + qoff + 128],
                                        rhs=kseq[:, 512 * kc:512 * kc + nn],
                                        start=True, stop=True)
                                P.op("tensor", smm,
                                     reads=["qbuf", "kseq"], writes=[rPA[hs]])
                                nmask = 256 if kc == KC - 1 else 0
                                ncopy = nn - nmask
                                if ncopy > 0:
                                    P.op("vector",
                                         (lambda hs=hs, kc=kc, nc=ncopy:
                                          (lambda e: e.tensor_copy(
                                              out=Sb[hs][:, 512 * kc:
                                                         512 * kc + nc],
                                              in_=PA[hs][:, 0:nc]))),
                                         reads=[rPA[hs]],
                                         writes=[Sr[hs]] if kc == 0 else [])
                                if nmask:
                                    P.op("vector",
                                         (lambda hs=hs, kc=kc, nn=nn:
                                          (lambda e: e.tensor_add(
                                              out=Sb[hs][:, 512 * kc + nn - 256:
                                                         512 * kc + nn],
                                              in0=PA[hs][:, nn - 256:nn],
                                              in1=mask))),
                                         reads=[rPA[hs], "c_mask"],
                                         writes=[Sr[hs]] if kc == 0 else [])
                            P.res[Sr[hs]] = ((("sDVE", P.vals["sDVE"],
                                               "vector"), []))
                            P.op("vector",
                                 (lambda hs=hs, L=L: (lambda e: e.reduce_max(
                                     out=m_c, in_=Sb[hs][:, 0:L], axis=AX.X))),
                                 reads=[Sr[hs]], writes=["mc"], strict=True)
                            P.op("vector", lambda e: e.tensor_scalar_mul(
                                out=nb_c, in0=m_c, scalar1=-SCALE),
                                reads=["mc"], writes=["nbc"])
                            P.op("scalar",
                                 (lambda hs=hs, L=L: (lambda e: e.activation(
                                     out=Pb[hs][:, 0:L], in_=Sb[hs][:, 0:L],
                                     func=AF.Exp, bias=nb_c, scale=SCALE))),
                                 reads=[Sr[hs], "nbc"], writes=[Pr[hs]])
                        for jp in range(0, NJ, 2):
                            tpar = (jp // 2) % 2
                            njc = min(2, NJ - jp)
                            def ptg(e, jp=jp, njc=njc, tpar=tpar):
                                for jj in range(njc):
                                    for hs2 in range(2):
                                        mm = e.transpose(
                                            out=PT[tpar][:, 256 * jj + 128 * hs2:
                                                         256 * jj + 128 * (hs2 + 1)],
                                            in_=Pb[hs2][:, 128 * (jp + jj):
                                                        128 * (jp + jj + 1)],
                                            identity=identb)
                                return mm
                            P.op("tensor", ptg,
                                 reads=[Pr[0], Pr[1], "c_identb"],
                                 writes=[rPT[tpar]])
                            P.op("scalar",
                                 (lambda tpar=tpar, njc=njc:
                                  (lambda e: e.copy(
                                      out=PTb[tpar][:, 0:256 * njc],
                                      in_=PT[tpar][:, 0:256 * njc]))),
                                 reads=[rPT[tpar]], writes=[f"PTbb{tpar}"])
                            def pvg(e, jp=jp, njc=njc, tpar=tpar, opar=opar,
                                    NJ=NJ):
                                for jj in range(njc):
                                    e.matmul(out=PY[opar][:, 0:256],
                                             lhsT=vtok[:, 128 * (jp + jj):
                                                       128 * (jp + jj + 1)],
                                             rhs=PTb[tpar][:, 256 * jj:
                                                          256 * (jj + 1)],
                                             start=(jp + jj == 0),
                                             stop=(jp + jj == NJ - 1))
                                    mm = e.matmul(
                                        out=PB[opar][0:1, 0:256],
                                        lhsT=onescb,
                                        rhs=PTb[tpar][:, 256 * jj:256 * (jj + 1)],
                                        start=(jp + jj == 0),
                                        stop=(jp + jj == NJ - 1))
                                return mm
                            wl = [rPY[opar], rPB[opar]] if jp == 0 else []
                            P.op("tensor", pvg,
                                 reads=[f"PTbb{tpar}", "vtok", "c_onescb"],
                                 writes=wl)
                            if jp == 0:
                                pass
                        P.res[rPY[opar]] = ((("sPE", P.vals["sPE"],
                                              "tensor"), []))
                        P.res[rPB[opar]] = ((("sPE", P.vals["sPE"],
                                              "tensor"), []))
                        P.op("vector",
                             (lambda opar=opar: (lambda e: e.tensor_copy(
                                 out=den_r, in_=PB[opar][0:1, 0:256]))),
                             reads=[rPB[opar]], writes=["r_den"])
                        P.op("vector", lambda e: e.reciprocal(
                            out=rr_r, in_=den_r), reads=["r_den"],
                            writes=["r_rr"], strict=True)
                        P.op("tensor",
                             (lambda opar=opar: (lambda e: e.matmul(
                                 out=PB[opar][:, 256:512], lhsT=onesrf,
                                 rhs=rr_r, start=True, stop=True))),
                             reads=["r_rr", "c_onesrf"], writes=[rPB[opar]])
                        P.op("vector",
                             (lambda opar=opar: (lambda e: e.tensor_copy(
                                 out=rbc[:, 0:256], in_=PB[opar][:, 256:512]))),
                             reads=[rPB[opar]], writes=["rbcb"])
                        for hs, h in enumerate((hA, hB)):
                            P.op("vector",
                                 (lambda opar=opar, hs=hs, h=h, qoff=qoff:
                                  (lambda e: e.tensor_mul(
                                      out=Ao[:, HT * h + qoff:
                                             HT * h + qoff + 128],
                                      in0=PY[opar][:, 128 * hs:128 * (hs + 1)],
                                      in1=rbc[:, 128 * hs:128 * (hs + 1)]))),
                                 reads=[rPY[opar], "rbcb"],
                                 writes=["Aob"] if (qb == 0 and hg == 0
                                                    and hs == 0) else [])
                P.res["Aob"] = (((("sDVE", P.vals["sDVE"], "vector", False)), []))
                for m in range(DC):
                    par = m % 2
                    sl = m % 2
                    load_slab("wo", m, 0, f"w2b{sl}", w2b[sl], 0)
                    def womm(e, par=par, sl=sl):
                        for c in range(DC):
                            mm = e.matmul(out=PA[par][:, 0:HT],
                                          lhsT=w2b[sl][:, 128 * c:128 * (c + 1)],
                                          rhs=Ao[:, HT * c:HT * (c + 1)],
                                          start=(c == 0), stop=(c == DC - 1))
                        return mm
                    P.op("tensor", womm, reads=[f"w2b{sl}", "Aob"],
                         writes=[rPA[par]])
                    P.op("scalar",
                         (lambda t=t, m=m, par=par, c0=c0:
                          (lambda e: e.dma_start(
                              out=xs[par][:, 0:HT],
                              in_=xmid[t, m][:, c0:c0 + HT]))),
                         reads=["xmid"], writes=[f"xs{par}"], dma=True)
                    P.op("vector",
                         (lambda par=par: (lambda e: e.tensor_add(
                             out=xs[par][:, 0:HT], in0=xs[par][:, 0:HT],
                             in1=PA[par][:, 0:HT]))),
                         reads=[rPA[par]], writes=[f"xs{par}"])
                    P.op("scalar",
                         (lambda t=t, m=m, par=par, c0=c0:
                          (lambda e: e.dma_start(
                              out=xmid[t, m][:, c0:c0 + HT],
                              in_=xs[par][:, 0:HT]))),
                         reads=[f"xs{par}"], writes=["xmid"], dma=True)

    consts = [(bcol, bcols_d, "c_bcol"), (identb, identb_d, "c_identb"),
              (identf, identf_d, "c_identf"), (identh, identh_d, "c_identh"),
              (onescb, onescb_d, "c_onescb"), (onesrf, onesrf_d, "c_onesrf"),
              (mask, mask_d, "c_mask"), (cost, cost_d, "c_tab"),
              (sint, sint_d, "c_tab"), (b2aug[0:17, :], b2aug_d, "c_b2aug"),
              (sel, sel_d, "c_sel"), (gbb, gbb_d, "c_gbb"),
              (gatep, gatep_d, "c_gatep")]
    def load_consts():
        for (dst, src, rn) in consts:
            P.op("gpsimd",
                 (lambda d=dst, s=src: (lambda e: e.dma_start(out=d, in_=s))),
                 writes=[rn], dma=True)
        P.op("vector", lambda e: e.memset(cwaug[:], 0.0), writes=["cwaug"])
        P.op("gpsimd", lambda e: e.dma_start(out=cwaug[16:17, :], in_=halfrow_d),
             reads=["cwaug"], writes=["cwaug"], dma=True)

    load_consts()
    ffn_phase(1, True, False)
    if MK_STOP != "f1":
        P.barrier()
        kv_pass()
        P.barrier()
        attn_pass()
        if MK_STOP == "full":
            P.barrier()
            load_consts()
            ffn_phase(2, False, True)

    if MK_STOP == "f1":
        src_res = "xmid"
    elif MK_STOP == "attn":
        src_res = "xmid"
    else:
        src_res = None
    if src_res is not None:
        P.barrier()
        for t in range(NT):
            for m in range(DC):
                P.op("gpsimd",
                     (lambda t=t, m=m: (lambda e: e.dma_start(
                         out=out_d[128 * m:128 * (m + 1), TT * t:TT * (t + 1)],
                         in_=xmid[t, m]))),
                     reads=["xmid"], writes=["out"], dma=True)

    w, rd = P.res.get("out", (None, []))
    P.op("gpsimd", lambda e: e.nop(), extra=[w] if w else [])
    P.build()
    return nc



def _host_pack(inputs):
    f32 = np.float32
    x = np.asarray(inputs["L0_attn_x"], f32)
    gamma = np.asarray(inputs["L0_attn_gamma"], f32)
    Wq = np.asarray(inputs["L0_attn_Wq"], f32)
    Wkv = np.asarray(inputs["L0_attn_Wkv"], f32)
    Wo = np.asarray(inputs["L0_attn_Wo"], f32)
    gate_w = np.asarray(inputs["L0_ffn_gate_w"], f32)
    gate_b = np.asarray(inputs["L0_ffn_gate_b"], f32)
    w1 = np.asarray(inputs["L0_ffn_w1"], f32)
    b1 = np.asarray(inputs["L0_ffn_b1"], f32)
    w2 = np.asarray(inputs["L0_ffn_w2"], f32)
    b2 = np.asarray(inputs["L0_ffn_b2"], f32)
    w3 = np.asarray(inputs["L0_ffn_w3"], f32)
    b3 = np.asarray(inputs["L0_ffn_b3"], f32)
    ws1 = np.asarray(inputs["L0_ffn_ws1"], f32)
    bs1 = np.asarray(inputs["L0_ffn_bs1"], f32)
    ws2 = np.asarray(inputs["L0_ffn_ws2"], f32)
    bs2 = np.asarray(inputs["L0_ffn_bs2"], f32)
    ws3 = np.asarray(inputs["L0_ffn_ws3"], f32)
    bs3 = np.asarray(inputs["L0_ffn_bs3"], f32)

    C = np.ascontiguousarray
    shared = dict(
        w1t=C(w1.transpose(0, 2, 1)), w3t=C(w3.transpose(0, 2, 1)),
        w2t=C(w2.transpose(0, 2, 1)),
        ws1t=C(ws1.T), ws3t=C(ws3.T), ws2t=C(ws2.T),
        wq=C(Wq), wkv=C(Wkv), wo=C(Wo),
    )
    bcols = np.zeros((128, BCOLS), f32)
    for e in range(E):
        for k in range(HC):
            bcols[:, b1col(e, k)] = b1[e, 128 * k:128 * (k + 1)]
            bcols[:, b3col(e, k)] = b3[e, 128 * k:128 * (k + 1)]
    for eh in range(2):
        for k in range(HC):
            bcols[:, b1col(E + eh, k)] = bs1[1408 * eh + 128 * k:
                                             1408 * eh + 128 * (k + 1)]
            bcols[:, b3col(E + eh, k)] = bs3[1408 * eh + 128 * k:
                                             1408 * eh + 128 * (k + 1)]
    for c in range(DC):
        bcols[:, GCOL0 + c] = gamma[128 * c:128 * (c + 1)]
    shared["bcols"] = bcols
    shared["b2aug"] = C(np.concatenate([b2, bs2[None, :]], 0))
    gp = gate_w.T.reshape(DC, 128, E).transpose(1, 0, 2).reshape(128, DC * E)
    shared["gatep"] = C(gp)
    shared["gbb"] = C(np.broadcast_to(gate_b, (128, E)))
    sel = np.zeros((32, NEXP * 128), f32)
    for e in range(NEXP):
        sel[min(e, 16), 128 * e:128 * (e + 1)] = 1.0
    shared["sel"] = sel
    shared["halfrow"] = np.full((1, TT), 0.5, f32)
    shared["identb"] = np.eye(128, dtype=ml_dtypes.bfloat16)
    shared["identf"] = np.eye(128, dtype=f32)
    shared["identh"] = (0.5 * np.eye(128)).astype(f32)
    shared["onescb"] = np.ones((128, 1), ml_dtypes.bfloat16)
    shared["onesrf"] = np.ones((1, 128), f32)

    tri = np.zeros((128, 128), f32)
    tri[np.triu_indices(128, 1)] = NEG
    full = np.full((128, 128), NEG, f32)
    zero = np.zeros((128, 128), f32)
    masks = {0: np.concatenate([tri, full], 1),
             1: np.concatenate([zero, tri], 1)}

    inv = (10000.0 ** (-(np.arange(0, DH, 2, dtype=f32) / DH))).astype(f32)
    tabs = {}
    for hlf in range(2):
        pos = local_positions(hlf).astype(f32)
        fr = pos[None, :] * inv[:, None]
        fr = np.concatenate([fr, fr], 0)
        ct = np.cos(fr).astype(f32)
        st = np.sin(fr).astype(f32)
        st[0:64, :] *= -1.0
        tabs[hlf] = (C(ct), C(st))

    percore = []
    for c in range(8):
        s, hlf = c // 2, c % 2
        idx = local_positions(hlf)
        d = dict(shared)
        d["x_in"] = C(x[s][idx].T)
        d["costab"], d["sintab"] = tabs[hlf]
        d["mask"] = masks[hlf]
        percore.append(d)
    return percore


_NC_CACHE = {}


def kernel(**inputs):
    percore = _host_pack(inputs)
    if 0 not in _NC_CACHE:
        _NC_CACHE[0] = build_nc()
    nc = _NC_CACHE[0]
    res = run_bass_kernel_spmd(nc, percore, core_ids=list(range(8)))
    out = np.empty((B, N, D), np.float32)
    for c in range(8):
        s, hlf = c // 2, c % 2
        idx = local_positions(hlf)
        out[s][idx] = res.results[c]["out"].T
    return out


# version 05 (final)
# speedup vs baseline: 1.0303x; 1.0303x over previous
import os
import numpy as np
import ml_dtypes
import concourse.bass as bass
import concourse.mybir as mybir
from concourse.bass_utils import run_bass_kernel_spmd

F32 = mybir.dt.float32
BF16 = mybir.dt.bfloat16
AF = mybir.ActivationFunctionType
ALU = mybir.AluOpType
AX = mybir.AxisListType

B, N, D = 4, 4096, 2048
DC = D // 128
E = 16
NEXP = 18
H = 1408
HC = H // 128
HS = 2816
TT = 512
NT = 4
TCORE = 2048
HT = 256
HEADS = 16
DH = 128
EPS = 1e-5
TOPK = 6
SCALE = DH ** -0.5
NEG = -1.0e38

def b1col(e, k): return e * HC + k
def b3col(e, k): return NEXP * HC + e * HC + k
GCOL0 = 2 * NEXP * HC
BCOLS = GCOL0 + DC

W13SLAB = 128 * 2 * D
W2SLAB = 128 * H
EXPSZ = HC * W13SLAB + DC * W2SLAB
QSLAB = 128 * D
OFF_WQ = NEXP * EXPSZ
OFF_WO = OFF_WQ + DC * QSLAB
WBF_ELEMS = OFF_WO + DC * QSLAB

ENGINES = ("sync", "tensor", "vector", "scalar", "gpsimd")
ESEM = {"sync": ("sW", 16), "tensor": ("sPE", 1), "vector": ("sDVE", 1),
        "scalar": ("sSC", 1), "gpsimd": ("sG", 1)}

MK_STOP = os.environ.get("MK_STOP", "full")
MK_NT = int(os.environ.get("MK_NT", "4"))
MK_NEXPD = int(os.environ.get("MK_NEXPD", str(NEXP)))
MK_DEBUG = os.environ.get("MK_DEBUG", "0") == "1"


def local_blocks(half):
    return list(range(half, 32, 2))


def local_positions(half):
    p = []
    for pb in local_blocks(half):
        p.extend(range(128 * pb, 128 * (pb + 1)))
    return np.array(p, dtype=np.int64)


class Prog:
    def __init__(self, nc):
        self.nc = nc
        self.ops = {e: [] for e in ENGINES}
        self.sems = {}
        self.vals = {}
        for e in ENGINES:
            s, _ = ESEM[e]
            self.sems[s] = self.nc.alloc_semaphore(s)
            self.vals[s] = 0
        self.sems["sCC"] = self.nc.alloc_semaphore("sCC")
        self.vals["sCC"] = 0
        self.res = {}
        self.hwm = {e: {} for e in ENGINES}

    def _ms(self, engine, dma):
        s, step = ESEM[engine]
        isdma = dma or engine == "sync"
        self.vals[s] += 16 if isdma else step
        return (s, self.vals[s], engine, isdma)

    def op(self, engine, fn, reads=(), writes=(), dma=False, extra=(),
           coll=False, strict=False):
        waits = []
        for r in writes:
            w, rd = self.res.get(r, (None, []))
            for ms in rd:
                waits.append(("war", ms))
            if w is not None:
                waits.append(("waw", w))
        for r in reads:
            w, rd = self.res.get(r, (None, []))
            if w is not None:
                waits.append(("raw", w))
        for ms in extra:
            if ms is not None:
                waits.append(("raw", ms))
        if coll:
            s = "sCC"
            self.vals[s] += 1
            ms = (s, self.vals[s], engine, False)
        else:
            ms = self._ms(engine, dma)
        if strict:
            ms = (ms[0], ms[1], ms[2], True)
        need = {}
        for kind, w_ in waits:
            if len(w_) == 3:
                s, v, weng = w_
                wdma = True
            else:
                s, v, weng, wdma = w_
            if v is None:
                continue
            if kind in ("war", "waw") and weng == engine and not wdma:
                continue
            need[s] = max(need.get(s, 0), v)
        fwaits = []
        for s, v in need.items():
            if self.hwm[engine].get(s, 0) >= v:
                continue
            self.hwm[engine][s] = v
            fwaits.append((self.sems[s], v))
        step = 16 if (dma or (engine == "sync")) else 1
        if coll:
            step = 1
        self.ops[engine].append((fn, fwaits, self.sems[ms[0]], step))
        for r in writes:
            self.res[r] = (ms, [])
        for r in reads:
            w, rd = self.res.get(r, (None, []))
            rd.append(ms)
            self.res[r] = (w, rd)
        return ms

    def barrier(self):
        tot = {ESEM[e][0]: self.vals[ESEM[e][0]] for e in ENGINES}
        for e in ENGINES:
            extra = [(s, v, "?", True) for s, v in tot.items() if s != ESEM[e][0]]
            if e in ("sync",):
                self.op(e, lambda eng: eng.nop(), extra=extra, dma=True)
            else:
                self.op(e, lambda eng: eng.nop(), extra=extra)
        for r in self.res:
            self.res[r] = (None, [])
        for e in ENGINES:
            pass

    def build(self):
        with self.nc.Block() as block:
            for eng in ENGINES:
                ops = self.ops[eng]
                if not ops:
                    continue
                def mk(ops):
                    def run(e):
                        for fn, waits, sem, step in ops:
                            for (sh, v) in waits:
                                e.wait_ge(sh, v)
                            co = fn.__code__
                            nd = len(fn.__defaults__ or ())
                            if co.co_argcount == nd:
                                ins = fn()(e)
                            else:
                                ins = fn(e)
                            if ins is not None:
                                ins.then_inc(sem, step)
                    return run
                getattr(block, eng)(mk(ops))


def build_nc():
    nc = bass.Bass("TRN2", target_bir_lowering=False)

    dram = lambda n, s, d: nc.dram_tensor(n, s, d, kind="ExternalInput").ap()
    x_in = dram("x_in", [D, TCORE], F32)
    w1t = dram("w1t", [E, D, H], F32)
    w3t = dram("w3t", [E, D, H], F32)
    w2t = dram("w2t", [E, H, D], F32)
    ws1t = dram("ws1t", [D, HS], F32)
    ws3t = dram("ws3t", [D, HS], F32)
    ws2t = dram("ws2t", [HS, D], F32)
    wq_d = dram("wq", [D, D], F32)
    wkv_d = dram("wkv", [D, 256], F32)
    wo_d = dram("wo", [D, D], F32)
    bcols_d = dram("bcols", [128, BCOLS], F32)
    b2aug_d = dram("b2aug", [17, D], F32)
    gatep_d = dram("gatep", [128, DC * E], F32)
    gbb_d = dram("gbb", [128, E], F32)
    sel_d = dram("sel", [32, NEXP * 128], F32)
    halfrow_d = dram("halfrow", [1, TT], F32)
    identb_d = dram("identb", [128, 128], BF16)
    identf_d = dram("identf", [128, 128], F32)
    identh_d = dram("identh", [128, 128], F32)
    onescb_d = dram("onescb", [128, 1], BF16)
    onesrf_d = dram("onesrf", [1, 128], F32)
    mask_d = dram("mask", [128, 256], F32)
    cost_d = dram("costab", [128, TCORE], F32)
    sint_d = dram("sintab", [128, TCORE], F32)

    out_d = nc.dram_tensor("out", [D, TCORE], F32, kind="ExternalOutput").ap()
    if MK_DEBUG:
        dbg_xn = nc.dram_tensor("dbg_xn", [128, DC * TT], BF16,
                                kind="ExternalOutput").ap()
        dbg_gtok = nc.dram_tensor("dbg_gtok", [128, 64], F32,
                                  kind="ExternalOutput").ap()
        dbg_cwaug = nc.dram_tensor("dbg_cwaug", [32, TT], F32,
                                   kind="ExternalOutput").ap()
        dbg_y = nc.dram_tensor("dbg_y", [128, DC * TT], F32,
                               kind="ExternalOutput").ap()
        dbg_prob = nc.dram_tensor("dbg_prob", [128, 64], F32,
                                  kind="ExternalOutput").ap()
        dbg_cwtok = nc.dram_tensor("dbg_cwtok", [128, 64], F32,
                                   kind="ExternalOutput").ap()
        dbg_bias = nc.dram_tensor("dbg_bias", [128, 64], F32,
                                  kind="ExternalOutput").ap()

    idram = lambda n, s, d: nc.dram_tensor(n, s, d, kind="Internal").ap()
    WSPLIT = 9 * EXPSZ
    wbf_a = idram("wbf_a", [WSPLIT], BF16)
    wbf_b = idram("wbf_b", [WBF_ELEMS - WSPLIT], BF16)
    class _WBF:
        def __getitem__(self, sl):
            a, b = sl.start, sl.stop
            if b <= WSPLIT:
                return wbf_a[a:b]
            assert a >= WSPLIT, (a, b)
            return wbf_b[a - WSPLIT:b - WSPLIT]
    wbf = _WBF()
    xmid = idram("xmid", [NT, DC, 128, TT], F32)
    xn2scr = idram("xn2scr", [NT, DC, 128, TT], BF16)
    kcon = idram("kcon", [128, TCORE], BF16)
    vcon = idram("vcon", [TCORE, 128], BF16)
    kg = idram("kg", [256, TCORE], BF16)
    vg = idram("vg", [2 * TCORE, 128], BF16)

    ns = nc.alloc_sbuf_tensor
    nat = nc.alloc_sbuf_tensor_at
    xn = ns("xn", [128, DC * TT], BF16).ap()
    w13cat = ns("w13cat", [128, 4 * 2 * D], BF16).ap()
    w13 = [w13cat[:, 4096 * i:4096 * (i + 1)] for i in range(4)]
    w2b = [ns(f"w2b_{i}", [128, 2 * H], BF16).ap() for i in range(2)]
    xs = [ns(f"xs_{i}", [128, TT], F32).ap() for i in range(2)]
    stage = [ns(f"stage{i}", [128, 2048], F32).ap() for i in range(2)]
    bcol = ns("sb_bcol", [128, BCOLS], F32).ap()
    identb = ns("sb_identb", [128, 128], BF16).ap()
    identf = ns("sb_identf", [128, 128], F32).ap()
    identh = ns("sb_identh", [128, 128], F32).ap()
    onescb = ns("sb_onescb", [128, 1], BF16).ap()
    onesrf = ns("sb_onesrf", [1, 128], F32).ap()
    mask = ns("sb_mask", [128, 256], F32).ap()
    cost = ns("cost", [128, TCORE], F32).ap()
    sint = ns("sint", [128, TCORE], F32).ap()
    srow = ns("srow", [1, 2560], F32).ap()
    mu_r = srow[0:1, 0:512]
    m2_r = srow[0:1, 512:1024]
    rst_r = srow[0:1, 1024:1536]
    tmp_r = srow[0:1, 1536:2048]
    den_r = srow[0:1, 2048:2304]
    rr_r = srow[0:1, 2304:2560]
    pcol = ns("pcol", [128, 8], F32).ap()
    m_c = pcol[:, 0:1]
    nb_c = pcol[:, 1:2]
    thr_c = pcol[:, 2:3]

    AR = 94000
    lo, hi = nc.bump_sbuf(AR)
    cur = [lo]
    def at(name, shape, dtype):
        sz = shape[1] * mybir.dt.size(dtype)
        sz = (sz + 31) // 32 * 32
        h = nat(name, shape, dtype, offset=cur[0])
        cur[0] += sz
        assert cur[0] <= hi, (name, cur[0], hi)
        return h.ap()
    y = at("y", [128, DC * TT], F32)
    hbuf = at("hbuf", [128, HC * TT], BF16)
    sbuf_s = [at(f"ss{i}", [128, TT], F32) for i in range(2)]
    tbuf = [at(f"tb{i}", [128, TT], F32) for i in range(2)]
    xcast = [at(f"xc{i}", [128, TT], BF16) for i in range(2)]
    xsq = [at(f"xq{i}", [128, TT], BF16) for i in range(2)]
    cwb = at("cwb", [128, TT], F32)
    cwaug = at("cwaug", [32, TT], F32)
    gsc = at("gsc", [32, TT], F32)
    gtok = at("gtok", [128, 64], F32)
    gprob = at("gprob", [128, 64], F32)
    gbias = at("gbias", [128, 64], F32)
    gping = at("gping", [128, 16], F32)
    gpong = at("gpong", [128, 16], F32)
    gmsk = at("gmsk", [128, 16], F32)
    gbb = at("sb_gbb", [128, E], F32)
    cwtok = at("cwtok", [128, 64], F32)
    b2aug = at("sb_b2aug", [32, D], F32)
    gatep = at("sb_gatep", [128, DC * E], BF16)
    sel = at("sb_sel", [32, NEXP * 128], F32)
    f_end = cur[0]
    cur[0] = lo
    S0 = at("S0", [128, 4096], F32)
    S1 = at("S1", [128, 4096], F32)
    P0 = at("P0", [128, 4096], BF16)
    P1 = at("P1", [128, 4096], BF16)
    kseq = at("kseq", [128, N], BF16)
    vtok = at("vtok", [128, N], BF16)
    krot = at("krot", [128, TT], F32)
    t1s = [at(f"t1s{i}", [128, TT], F32) for i in range(2)]
    sub0 = cur[0]
    xn1h = at("xn1h", [128, DC * HT], F32)
    axc = [at(f"axc{i}", [128, HT], BF16) for i in range(2)]
    asq = [at(f"asq{i}", [128, HT], BF16) for i in range(2)]
    kvb = at("kvb", [128, TT], BF16)
    vtb = at("vtb", [128, TT], BF16)
    kv_end = cur[0]
    cur[0] = sub0
    qbuf = at("qbuf", [128, HEADS * HT], BF16)
    Ao = at("Ao", [128, HEADS * HT], BF16)
    PTb = [at(f"PTb{i}", [128, 512], BF16) for i in range(2)]
    rbc = at("rbc", [128, 512], F32)
    at_end = cur[0]
    a_end = max(kv_end, at_end)
    assert max(f_end, a_end) <= hi, (f_end, a_end, hi)

    nps = nc.alloc_psum_tensor
    PA = [nps(f"PA{i}", [128, 512], F32).ap() for i in range(2)]
    PB = [nps(f"PB{i}", [128, 512], F32).ap() for i in range(2)]
    PY = [nps(f"PY{i}", [128, 512], F32).ap() for i in range(2)]
    PT = [nps(f"PT{i}", [128, 1024], BF16).ap() for i in range(2)]
    rPA = ["PA0", "PA1"]; rPB = ["PB0", "PB1"]; rPY = ["PY0", "PY1"]
    rPT = ["PT0", "PT1"]

    P = Prog(nc)
    first_done = set()
    state = dict(stage_i=0, cast_i=0)
    cast_cycle = ["vector", "gpsimd", "scalar"]

    def slab_off(kind, a, b=0):
        if kind == "w13":
            return a * EXPSZ + b * W13SLAB
        if kind == "w2":
            return a * EXPSZ + HC * W13SLAB + b * W2SLAB
        if kind == "wq":
            return OFF_WQ + a * QSLAB
        return OFF_WO + a * QSLAB

    def f32_src(kind, a, b=0):
        if kind == "w13":
            e, k = a, b
            if e < E:
                s1 = w1t[e, :, 128 * k:128 * (k + 1)]
                s3 = w3t[e, :, 128 * k:128 * (k + 1)]
            else:
                off = (e - E) * H
                s1 = ws1t[:, off + 128 * k: off + 128 * (k + 1)]
                s3 = ws3t[:, off + 128 * k: off + 128 * (k + 1)]
            r = lambda ap: ap.rearrange("(c p) j -> p c j", p=128)
            return [(r(s1), 2048), (r(s3), 2048)]
        if kind == "w2":
            e, m = a, b
            if e < E:
                s = w2t[e, :, 128 * m:128 * (m + 1)]
            else:
                off = (e - E) * H
                s = ws2t[off:off + H, 128 * m:128 * (m + 1)]
            return [(s.rearrange("(c p) j -> p c j", p=128), H)]
        s = (wq_d if kind == "wq" else wo_d)[:, 128 * a:128 * (a + 1)]
        return [(s.rearrange("(c p) j -> p c j", p=128), 2048)]

    def load_slab(kind, a, b, dst, dst_ap, dst_off):
        key = (kind, a, b)
        off = slab_off(kind, a, b)
        if key in first_done:
            halves = f32_src(kind, a, b)
            total = sum(h[1] for h in halves)
            P.op("sync",
                 (lambda d=dst_ap, o=dst_off, t=total, woff=off:
                  (lambda sy: sy.dma_start(
                      out=d[:, o:o + t],
                      in_=wbf[woff:woff + 128 * t].rearrange("(p j) -> p j", p=128)))),
                 reads=[("wbf", key)], writes=[dst], dma=True)
            return
        first_done.add(key)
        o = dst_off
        for hi_, (src_ap, elems) in enumerate(f32_src(kind, a, b)):
            sp = state["stage_i"] % 2
            state["stage_i"] += 1
            P.op("sync",
                 (lambda s=src_ap, sp=sp, el=elems:
                  (lambda sy: sy.dma_start(out=stage[sp][:, 0:el], in_=s))),
                 writes=[f"stage{sp}"], dma=True)
            eng = cast_cycle[state["cast_i"] % 3]
            state["cast_i"] += 1
            if eng == "scalar":
                castfn = (lambda d=dst_ap, oo=o, sp=sp, el=elems:
                          (lambda e: e.copy(out=d[:, oo:oo + el],
                                            in_=stage[sp][:, 0:el])))
            else:
                castfn = (lambda d=dst_ap, oo=o, sp=sp, el=elems:
                          (lambda e: e.tensor_copy(out=d[:, oo:oo + el],
                                                   in_=stage[sp][:, 0:el])))
            P.op(eng, castfn, reads=[f"stage{sp}"], writes=[dst])
            tot = sum(h2[1] for h2 in f32_src(kind, a, b))
            P.op("sync",
                 (lambda d=dst_ap, oo=o, el=elems, tot=tot,
                         o0=dst_off, woff=off:
                  (lambda sy: sy.dma_start(
                      out=wbf[woff:woff + 128 * tot].rearrange(
                          "(p j) -> p j", p=128)[:, oo - o0:oo - o0 + el],
                      in_=d[:, oo:oo + el]))),
                 reads=[dst], writes=[("wbf", key)], dma=True)
            o += elems

    def ffn_phase(sub, src_ext, dst_out):
        def xsrc(t, c):
            if src_ext:
                return x_in[128 * c:128 * (c + 1), TT * t:TT * (t + 1)]
            return xmid[t, c]
        def xdst(t, c):
            if dst_out:
                return out_d[128 * c:128 * (c + 1), TT * t:TT * (t + 1)]
            return xmid[t, c]
        rsrc = "x_ext" if src_ext else "xmid"

        for t in range(MK_NT):
            for c in range(DC):
                ci = c % 2
                P.op("scalar",
                     (lambda t=t, c=c, ci=ci:
                      (lambda e: e.dma_start(out=xs[ci][:], in_=xsrc(t, c)))),
                     reads=[rsrc], writes=[f"xs{ci}"], dma=True)
                P.op("gpsimd",
                     (lambda ci=ci: (lambda e: e.tensor_copy(
                         out=xcast[ci][:], in_=xs[ci][:]))),
                     reads=[f"xs{ci}"], writes=[f"xc{ci}"])
                P.op("scalar",
                     (lambda ci=ci: (lambda e: e.activation(
                         out=xsq[ci][:], in_=xs[ci][:], func=AF.Square))),
                     reads=[f"xs{ci}"], writes=[f"xq{ci}"])
                P.op("tensor",
                     (lambda ci=ci, c=c: (lambda e: e.matmul(
                         out=PY[0][0:1, :], lhsT=onescb, rhs=xcast[ci][:],
                         start=(c == 0), stop=(c == DC - 1)))),
                     reads=[f"xc{ci}", "c_onescb"],
                     writes=["PY0"] if c == 0 else [], )
                if c > 0:
                    P.res["PY0"] = (((("sPE", P.vals["sPE"], "tensor", False)), []))
                P.op("tensor",
                     (lambda ci=ci, c=c: (lambda e: e.matmul(
                         out=PY[1][0:1, :], lhsT=onescb, rhs=xsq[ci][:],
                         start=(c == 0), stop=(c == DC - 1)))),
                     reads=[f"xq{ci}"], writes=["PY1"] if c == 0 else [])
                if c > 0:
                    P.res["PY1"] = (((("sPE", P.vals["sPE"], "tensor", False)), []))
            P.op("vector", lambda e: e.tensor_scalar_mul(
                out=mu_r, in0=PY[0][0:1, :], scalar1=1.0 / D),
                reads=["PY0"], writes=["r_mu"])
            P.op("vector", lambda e: e.tensor_scalar_mul(
                out=m2_r, in0=PY[1][0:1, :], scalar1=1.0 / D),
                reads=["PY1"], writes=["r_m2"])
            P.op("vector", lambda e: e.tensor_mul(
                out=tmp_r, in0=mu_r, in1=mu_r), reads=["r_mu"], writes=["r_tmp"])
            P.op("vector", lambda e: e.scalar_tensor_tensor(
                out=m2_r, in0=m2_r, scalar=EPS, in1=tmp_r,
                op0=ALU.add, op1=ALU.subtract),
                reads=["r_tmp"], writes=["r_m2"])
            P.op("scalar", lambda e: e.activation(
                out=tmp_r, in_=m2_r, func=AF.Ln),
                reads=["r_m2"], writes=["r_tmp"])
            P.op("scalar", lambda e: e.activation(
                out=rst_r, in_=tmp_r, func=AF.Exp, scale=-0.5),
                reads=["r_tmp"], writes=["r_rst"])
            P.op("tensor", lambda e: e.matmul(
                out=PB[0][:], lhsT=onesrf, rhs=mu_r, start=True, stop=True),
                reads=["r_mu", "c_onesrf"], writes=["PB0"])
            P.op("tensor", lambda e: e.matmul(
                out=PB[1][:], lhsT=onesrf, rhs=rst_r, start=True, stop=True),
                reads=["r_rst"], writes=["PB1"])
            for c in range(DC):
                ci = c % 2
                P.op("scalar",
                     (lambda t=t, c=c, ci=ci:
                      (lambda e: e.dma_start(out=xs[ci][:], in_=xsrc(t, c)))),
                     reads=[rsrc], writes=[f"xs{ci}"], dma=True)
                P.op("vector",
                     (lambda ci=ci: (lambda e: e.tensor_sub(
                         out=tbuf[ci][:], in0=xs[ci][:], in1=PB[0][:]))),
                     reads=[f"xs{ci}", "PB0"], writes=[f"tb{ci}"])
                P.op("vector",
                     (lambda ci=ci, c=c: (lambda e: e.scalar_tensor_tensor(
                         out=xn[:, TT * c:TT * (c + 1)], in0=tbuf[ci][:],
                         scalar=1.0, in1=PB[1][:], op0=ALU.mult, op1=ALU.mult))),
                     reads=[f"tb{ci}", "PB1"], writes=["xn"] if c == 0 else [])
            ms_norm = ("sDVE", P.vals["sDVE"], "vector", False)
            P.res["xn"] = (ms_norm, [])
            for c in range(DC):
                P.op("tensor",
                     (lambda c=c: (lambda e: e.matmul(
                         out=PA[0][0:16, :], lhsT=gatep[:, E * c:E * (c + 1)],
                         rhs=xn[:, TT * c:TT * (c + 1)],
                         start=(c == 0), stop=(c == DC - 1)))),
                     reads=["xn", "c_gatep"], writes=["PA0"] if c == 0 else [])
            P.res["PA0"] = (((("sPE", P.vals["sPE"], "tensor", False)), []))
            P.op("vector", lambda e: e.tensor_copy(
                out=gsc[0:16, :], in_=PA[0][0:16, :]),
                reads=["PA0"], writes=["gsc"])
            for j in range(4):
                P.op("tensor",
                     (lambda j=j: (lambda e: e.matmul(
                         out=PA[1][:, 16 * j:16 * (j + 1)],
                         lhsT=gsc[0:16, 128 * j:128 * (j + 1)],
                         rhs=identf[0:16, 0:16], start=True, stop=True))),
                     reads=["gsc", "c_identf"],
                     writes=["PA1"] if j == 0 else [])
            P.res["PA1"] = (((("sPE", P.vals["sPE"], "tensor", False)), []))
            P.op("vector", lambda e: e.tensor_copy(out=gtok[:], in_=PA[1][:, 0:64]),
                 reads=["PA1"], writes=["gtok"])
            for j in range(4):
                gt = gtok[:, 16 * j:16 * (j + 1)]
                pj = gprob[:, 16 * j:16 * (j + 1)]
                bj = gbias[:, 16 * j:16 * (j + 1)]
                cj = cwtok[:, 16 * j:16 * (j + 1)]
                P.op("vector", (lambda gt=gt: (lambda e: e.reduce_max(
                    out=m_c, in_=gt, axis=AX.X))), reads=["gtok"], writes=["mc"], strict=True)
                P.op("vector", lambda e: e.tensor_scalar_mul(
                    out=m_c, in0=m_c, scalar1=-1.0), reads=["mc"], writes=["mc"])
                P.op("scalar",
                     (lambda gt=gt, pj=pj: (lambda e: e.activation(
                         out=pj, in_=gt, func=AF.Exp, bias=m_c))),
                     reads=["gtok", "mc"], writes=["gprob"])
                P.op("vector", (lambda pj=pj: (lambda e: e.reduce_sum(
                    out=m_c, in_=pj, axis=AX.X))), reads=["gprob"], writes=["mc"], strict=True)
                P.op("vector", lambda e: e.reciprocal(out=nb_c, in_=m_c),
                     reads=["mc"], writes=["nbc"], strict=True)
                P.op("vector", (lambda pj=pj: (lambda e: e.tensor_scalar_mul(
                    out=pj, in0=pj, scalar1=nb_c))),
                    reads=["nbc", "gprob"], writes=["gprob"])
                P.op("vector", (lambda pj=pj, bj=bj: (lambda e: e.tensor_add(
                    out=bj, in0=pj, in1=gbb))),
                    reads=["gprob", "c_gbb"], writes=["gbias"])
                P.op("vector", (lambda bj=bj: (lambda e: e.tensor_copy(
                    out=gping[:], in_=bj))), reads=["gbias"], writes=["gping"])
                cur_b, cur_r, nxt_b, nxt_r = gping, "gping", gpong, "gpong"
                for it in range(TOPK - 1):
                    P.op("vector", (lambda cb=cur_b: (lambda e: e.reduce_max(
                        out=thr_c, in_=cb, axis=AX.X))),
                        reads=[cur_r], writes=["thrc"], strict=True)
                    P.op("vector", (lambda cb=cur_b: (lambda e: e.tensor_scalar(
                        out=gmsk[:], in0=cb, scalar1=thr_c, scalar2=NEG,
                        op0=ALU.is_ge, op1=ALU.mult))),
                        reads=[cur_r, "thrc"], writes=["gmsk"])
                    P.op("vector", (lambda cb=cur_b, nb=nxt_b:
                                    (lambda e: e.tensor_add(
                                        out=nb, in0=cb, in1=gmsk[:]))),
                         reads=[cur_r, "gmsk"], writes=[nxt_r])
                    cur_b, cur_r, nxt_b, nxt_r = nxt_b, nxt_r, cur_b, cur_r
                P.op("vector", (lambda cb=cur_b: (lambda e: e.reduce_max(
                    out=thr_c, in_=cb, axis=AX.X))), reads=[cur_r],
                    writes=["thrc"], strict=True)
                P.op("vector", (lambda bj=bj: (lambda e: e.tensor_single_scalar(
                    out=gmsk[:], in_=bj, scalar=thr_c, op=ALU.is_ge))),
                    reads=["gbias", "thrc"], writes=["gmsk"])
                P.op("vector", (lambda pj=pj, cj=cj: (lambda e: e.tensor_mul(
                    out=cj, in0=pj, in1=gmsk[:]))),
                    reads=["gprob", "gmsk"], writes=["cwtok"] if j == 3 else [])
            P.res["cwtok"] = (((("sDVE", P.vals["sDVE"], "vector", False)), []))
            for j in range(4):
                P.op("tensor",
                     (lambda j=j: (lambda e: e.matmul(
                         out=PB[0][0:16, 128 * j:128 * (j + 1)],
                         lhsT=cwtok[:, 16 * j:16 * (j + 1)],
                         rhs=identh, start=True, stop=True))),
                     reads=["cwtok", "c_identh"],
                     writes=["PB0"] if j == 0 else [])
            P.res["PB0"] = (((("sPE", P.vals["sPE"], "tensor", False)), []))
            P.op("vector", lambda e: e.tensor_copy(
                out=cwaug[0:16, :], in_=PB[0][0:16, :]),
                reads=["PB0"], writes=["cwaug"])
            for m in range(DC):
                mp = m % 2
                P.op("tensor",
                     (lambda m=m, mp=mp: (lambda e: e.matmul(
                         out=PY[mp][:], lhsT=b2aug[0:17, 128 * m:128 * (m + 1)],
                         rhs=cwaug[0:17, :], start=True, stop=True))),
                     reads=["cwaug", "c_b2aug"], writes=[rPY[mp]])
                P.op("vector",
                     (lambda m=m, mp=mp: (lambda e: e.tensor_copy(
                         out=y[:, TT * m:TT * (m + 1)], in_=PY[mp][:]))),
                     reads=[rPY[mp]], writes=["y"] if m == 0 else [])
            P.res["y"] = (((("sDVE", P.vals["sDVE"], "vector", False)), []))

            for ex in range(MK_NEXPD):
                gidx0 = ((sub - 1) * NT + t) * NEXP + ex
                kpar0 = (gidx0 * HC) % 2
                ub = (kpar0 + 1) % 2
                P.op("tensor",
                     (lambda ex=ex, ub=ub: (lambda e: e.matmul(
                         out=PA[ub][:], lhsT=sel[:, 128 * ex:128 * (ex + 1)],
                         rhs=cwaug[0:32, :], start=True, stop=True))),
                     reads=["cwaug", "c_sel"], writes=[rPA[ub]])
                P.op("vector",
                     (lambda ub=ub: (lambda e: e.tensor_copy(
                         out=cwb[:], in_=PA[ub][:]))),
                     reads=[rPA[ub]], writes=["cwb"])
                first_ex = ("w13", ex, 0) not in first_done
                def dual_load(k0, ex=ex):
                    nsl = min(2, HC - k0)
                    sl0 = k0 % 4
                    off = slab_off("w13", ex, k0)
                    qeng = "sync" if (k0 // 2) % 2 == 0 else "scalar"
                    wr = [f"w13b{sl0 + u}" for u in range(nsl)]
                    rd = [("wbf", ("w13", ex, k0 + u)) for u in range(nsl)]
                    if nsl == 2:
                        P.op(qeng,
                             (lambda sl0=sl0, woff=off:
                              (lambda q: q.dma_start(
                                  out=w13cat[:, 4096 * sl0:4096 * (sl0 + 2)],
                                  in_=wbf[woff:woff + 2 * 128 * 4096].rearrange(
                                      "(s p j) -> p s j", s=2, p=128)))),
                             reads=rd, writes=wr, dma=True)
                    else:
                        P.op(qeng,
                             (lambda sl0=sl0, woff=off:
                              (lambda q: q.dma_start(
                                  out=w13cat[:, 4096 * sl0:4096 * (sl0 + 1)],
                                  in_=wbf[woff:woff + 128 * 4096].rearrange(
                                      "(p j) -> p j", p=128)))),
                             reads=rd, writes=wr, dma=True)
                if not first_ex:
                    dual_load(0)
                    dual_load(2)
                for k in range(HC):
                    gk = gidx0 * HC + k
                    par = gk % 2
                    sl = k % 4
                    if first_ex:
                        load_slab("w13", ex, k, f"w13b{sl}", w13[sl], 0)
                    def agrp(e, par=par, sl=sl):
                        for c in range(DC):
                            mm = e.matmul(out=PA[par][:],
                                          lhsT=w13[sl][:, 128 * c:128 * (c + 1)],
                                          rhs=xn[:, TT * c:TT * (c + 1)],
                                          start=(c == 0), stop=(c == DC - 1))
                        return mm
                    P.op("tensor", agrp, reads=[f"w13b{sl}", "xn"],
                         writes=[rPA[par]])
                    def bgrp(e, par=par, sl=sl):
                        for c in range(DC):
                            mm = e.matmul(out=PB[par][:],
                                          lhsT=w13[sl][:, D + 128 * c:D + 128 * (c + 1)],
                                          rhs=xn[:, TT * c:TT * (c + 1)],
                                          start=(c == 0), stop=(c == DC - 1))
                        return mm
                    P.op("tensor", bgrp, reads=[f"w13b{sl}", "xn"],
                         writes=[rPB[par]])
                    if (not first_ex) and k in (1, 3, 5, 7):
                        dual_load(k + 3)
                    col = b1col(ex, k)
                    P.op("scalar",
                         (lambda par=par, col=col: (lambda e: e.activation(
                             out=sbuf_s[par][:], in_=PA[par][:], func=AF.Silu,
                             bias=bcol[:, col:col + 1]))),
                         reads=[rPA[par], "c_bcol"], writes=[f"ss{par}"])
                    col = b3col(ex, k)
                    P.op("vector",
                         (lambda par=par, col=col: (lambda e:
                          e.scalar_tensor_tensor(
                              out=tbuf[par][:], in0=PB[par][:],
                              scalar=bcol[:, col:col + 1], in1=sbuf_s[par][:],
                              op0=ALU.add, op1=ALU.mult))),
                         reads=[rPB[par], f"ss{par}", "c_bcol"],
                         writes=[f"tb{par}"])
                    P.op("gpsimd",
                         (lambda par=par, k=k: (lambda e: e.tensor_mul(
                             out=hbuf[:, TT * k:TT * (k + 1)],
                             in0=tbuf[par][:], in1=cwb[:]))),
                         reads=[f"tb{par}", "cwb"],
                         writes=["hbuf"] if k == 0 else [])
                P.res["hbuf"] = (((("sG", P.vals["sG"], "gpsimd", False)), []))
                for mp in range(DC // 2):
                    gp_ = gidx0 * (DC // 2) + mp
                    w2i = gp_ % 2
                    key = ("w2", ex, 2 * mp)
                    if key in first_done:
                        off = slab_off("w2", ex, 2 * mp)
                        P.op("sync",
                             (lambda bb=w2b[w2i], woff=off:
                              (lambda sy: sy.dma_start(
                                  out=bb[:],
                                  in_=wbf[woff:woff + 2 * 128 * H].rearrange(
                                      "(m p j) -> p m j", m=2, p=128)))),
                             reads=[("wbf", ("w2", ex, 2 * mp)),
                                    ("wbf", ("w2", ex, 2 * mp + 1))],
                             writes=[f"w2b{w2i}"], dma=True)
                    else:
                        first_done.add(key)
                        first_done.discard(key)
                        for m2 in range(2):
                            load_slab("w2", ex, 2 * mp + m2,
                                      f"w2b{w2i}", w2b[w2i], m2 * H)
                        first_done.add(key)
                    for m2 in range(2):
                        m = 2 * mp + m2
                        def y2g(e, m=m, w2i=w2i):
                            off = (m % 2) * H
                            for k in range(HC):
                                mm = e.matmul(
                                    out=PY[m % 2][:],
                                    lhsT=w2b[w2i][:, off + 128 * k:
                                                  off + 128 * (k + 1)],
                                    rhs=hbuf[:, TT * k:TT * (k + 1)],
                                    start=(k == 0), stop=(k == HC - 1))
                            return mm
                        P.op("tensor", y2g, reads=[f"w2b{w2i}", "hbuf"],
                             writes=[rPY[m % 2]])
                        P.op("vector",
                             (lambda m=m: (lambda e: e.tensor_add(
                                 out=y[:, TT * m:TT * (m + 1)],
                                 in0=y[:, TT * m:TT * (m + 1)],
                                 in1=PY[m % 2][:]))),
                             reads=[rPY[m % 2], "y"], writes=["y"])

            if MK_DEBUG and sub == 1 and t == 0:
                P.op("gpsimd", lambda e: e.dma_start(out=dbg_xn, in_=xn[:]),
                     reads=["xn"], writes=["dbg"], dma=True)
                P.op("gpsimd", lambda e: e.dma_start(out=dbg_gtok, in_=gtok[:]),
                     reads=["gtok"], writes=["dbg"], dma=True)
                P.op("gpsimd", lambda e: e.dma_start(out=dbg_prob, in_=gprob[:]),
                     reads=["gprob"], writes=["dbg"], dma=True)
                P.op("gpsimd", lambda e: e.dma_start(out=dbg_cwaug, in_=cwaug[:]),
                     reads=["cwaug"], writes=["dbg"], dma=True)
                P.op("gpsimd", lambda e: e.dma_start(out=dbg_cwtok, in_=cwtok[:]),
                     reads=["cwtok"], writes=["dbg"], dma=True)
                P.op("gpsimd", lambda e: e.dma_start(out=dbg_bias, in_=gbias[:]),
                     reads=["gbias"], writes=["dbg"], dma=True)
                P.op("gpsimd", lambda e: e.dma_start(out=dbg_y, in_=y[:]),
                     reads=["y"], writes=["dbg"], dma=True)
            for m in range(DC):
                ci = m % 2
                P.op("scalar",
                     (lambda t=t, m=m, ci=ci:
                      (lambda e: e.dma_start(out=xs[ci][:], in_=xsrc(t, m)))),
                     reads=[rsrc], writes=[f"xs{ci}"], dma=True)
                P.op("vector",
                     (lambda m=m, ci=ci: (lambda e: e.tensor_add(
                         out=xs[ci][:], in0=xs[ci][:],
                         in1=y[:, TT * m:TT * (m + 1)]))),
                     reads=["y"], writes=[f"xs{ci}"])
                P.op("scalar",
                     (lambda t=t, m=m, ci=ci:
                      (lambda e: e.dma_start(out=xdst(t, m), in_=xs[ci][:]))),
                     reads=[f"xs{ci}"],
                     writes=["out" if dst_out else "xmid"], dma=True)

    def kv_pass():
        for mslab in range(2):
            sp = state["stage_i"] % 2
            state["stage_i"] += 1
            src = wkv_d[:, 128 * mslab:128 * (mslab + 1)].rearrange(
                "(c p) j -> p c j", p=128)
            P.op("sync",
                 (lambda s=src, sp=sp: (lambda sy: sy.dma_start(
                     out=stage[sp][:], in_=s))),
                 writes=[f"stage{sp}"], dma=True)
            P.op("vector",
                 (lambda sp=sp, mslab=mslab: (lambda e: e.tensor_copy(
                     out=w13[0][:, 2048 * mslab:2048 * (mslab + 1)],
                     in_=stage[sp][:]))),
                 reads=[f"stage{sp}"], writes=["w13b0"])

        for t in range(NT):
            for hh in range(2):
                c0 = HT * hh
                for c in range(DC):
                    ci = c % 2
                    P.op("scalar",
                         (lambda t=t, c=c, ci=ci, c0=c0:
                          (lambda e: e.dma_start(
                              out=xs[ci][:, 0:HT],
                              in_=xmid[t, c][:, c0:c0 + HT]))),
                         reads=["xmid"], writes=[f"xs{ci}"], dma=True)
                    P.op("gpsimd",
                         (lambda ci=ci: (lambda e: e.tensor_copy(
                             out=axc[ci][:], in_=xs[ci][:, 0:HT]))),
                         reads=[f"xs{ci}"], writes=[f"axc{ci}"])
                    P.op("scalar",
                         (lambda ci=ci: (lambda e: e.activation(
                             out=asq[ci][:], in_=xs[ci][:, 0:HT],
                             func=AF.Square))),
                         reads=[f"xs{ci}"], writes=[f"asq{ci}"])
                    P.op("tensor",
                         (lambda ci=ci, c=c: (lambda e: e.matmul(
                             out=PY[0][0:1, 0:HT], lhsT=onescb, rhs=axc[ci][:],
                             start=(c == 0), stop=(c == DC - 1)))),
                         reads=[f"axc{ci}", "c_onescb"],
                         writes=["PY0"] if c == 0 else [])
                    if c > 0:
                        P.res["PY0"] = (((("sPE", P.vals["sPE"], "tensor", False)), []))
                    P.op("tensor",
                         (lambda ci=ci, c=c: (lambda e: e.matmul(
                             out=PY[1][0:1, 0:HT], lhsT=onescb, rhs=asq[ci][:],
                             start=(c == 0), stop=(c == DC - 1)))),
                         reads=[f"asq{ci}"], writes=["PY1"] if c == 0 else [])
                    if c > 0:
                        P.res["PY1"] = (((("sPE", P.vals["sPE"], "tensor", False)), []))
                def statmath():
                    P.op("vector", lambda e: e.tensor_scalar_mul(
                        out=mu_r[0:1, 0:HT], in0=PY[0][0:1, 0:HT],
                        scalar1=1.0 / D), reads=["PY0"], writes=["r_mu"])
                    P.op("vector", lambda e: e.tensor_scalar_mul(
                        out=m2_r[0:1, 0:HT], in0=PY[1][0:1, 0:HT],
                        scalar1=1.0 / D), reads=["PY1"], writes=["r_m2"])
                    P.op("vector", lambda e: e.tensor_mul(
                        out=tmp_r[0:1, 0:HT], in0=mu_r[0:1, 0:HT],
                        in1=mu_r[0:1, 0:HT]), reads=["r_mu"], writes=["r_tmp"])
                    P.op("vector", lambda e: e.scalar_tensor_tensor(
                        out=m2_r[0:1, 0:HT], in0=m2_r[0:1, 0:HT], scalar=EPS,
                        in1=tmp_r[0:1, 0:HT], op0=ALU.add, op1=ALU.subtract),
                        reads=["r_tmp"], writes=["r_m2"])
                    P.op("scalar", lambda e: e.activation(
                        out=tmp_r[0:1, 0:HT], in_=m2_r[0:1, 0:HT], func=AF.Ln),
                        reads=["r_m2"], writes=["r_tmp"])
                    P.op("scalar", lambda e: e.activation(
                        out=rst_r[0:1, 0:HT], in_=tmp_r[0:1, 0:HT],
                        func=AF.Exp, scale=-0.5),
                        reads=["r_tmp"], writes=["r_rst"])
                    P.op("tensor", lambda e: e.matmul(
                        out=PB[0][:, 0:HT], lhsT=onesrf, rhs=mu_r[0:1, 0:HT],
                        start=True, stop=True),
                        reads=["r_mu", "c_onesrf"], writes=["PB0"])
                    P.op("tensor", lambda e: e.matmul(
                        out=PB[1][:, 0:HT], lhsT=onesrf, rhs=rst_r[0:1, 0:HT],
                        start=True, stop=True), reads=["r_rst"], writes=["PB1"])
                statmath()
                for c in range(DC):
                    ci = c % 2
                    P.op("scalar",
                         (lambda t=t, c=c, ci=ci, c0=c0:
                          (lambda e: e.dma_start(
                              out=xs[ci][:, 0:HT],
                              in_=xmid[t, c][:, c0:c0 + HT]))),
                         reads=["xmid"], writes=[f"xs{ci}"], dma=True)
                    P.op("vector",
                         (lambda ci=ci: (lambda e: e.tensor_sub(
                             out=t1s[ci][:, 0:HT], in0=xs[ci][:, 0:HT],
                             in1=PB[0][:, 0:HT]))),
                         reads=[f"xs{ci}", "PB0"], writes=[f"t1s{ci}"])
                    P.op("vector",
                         (lambda ci=ci, c=c: (lambda e: e.scalar_tensor_tensor(
                             out=xn1h[:, HT * c:HT * (c + 1)],
                             in0=t1s[ci][:, 0:HT], scalar=1.0,
                             in1=PB[1][:, 0:HT], op0=ALU.mult, op1=ALU.mult))),
                         reads=[f"t1s{ci}", "PB1"],
                         writes=["xn1h"] if c == 0 else [])
                P.res["xn1h"] = (((("sDVE", P.vals["sDVE"], "vector", False)), []))
                for c in range(DC):
                    ci = c % 2
                    P.op("gpsimd",
                         (lambda ci=ci, c=c: (lambda e: e.tensor_copy(
                             out=axc[ci][:], in_=xn1h[:, HT * c:HT * (c + 1)]))),
                         reads=["xn1h"], writes=[f"axc{ci}"])
                    P.op("scalar",
                         (lambda ci=ci, c=c: (lambda e: e.activation(
                             out=asq[ci][:], in_=xn1h[:, HT * c:HT * (c + 1)],
                             func=AF.Square))),
                         reads=["xn1h"], writes=[f"asq{ci}"])
                    P.op("tensor",
                         (lambda ci=ci, c=c: (lambda e: e.matmul(
                             out=PY[0][0:1, 0:HT], lhsT=onescb, rhs=axc[ci][:],
                             start=(c == 0), stop=(c == DC - 1)))),
                         reads=[f"axc{ci}", "c_onescb"],
                         writes=["PY0"] if c == 0 else [])
                    if c > 0:
                        P.res["PY0"] = (((("sPE", P.vals["sPE"], "tensor", False)), []))
                    P.op("tensor",
                         (lambda ci=ci, c=c: (lambda e: e.matmul(
                             out=PY[1][0:1, 0:HT], lhsT=onescb, rhs=asq[ci][:],
                             start=(c == 0), stop=(c == DC - 1)))),
                         reads=[f"asq{ci}"], writes=["PY1"] if c == 0 else [])
                    if c > 0:
                        P.res["PY1"] = (((("sPE", P.vals["sPE"], "tensor", False)), []))
                statmath()
                for c in range(DC):
                    ci = c % 2
                    P.op("vector",
                         (lambda ci=ci, c=c: (lambda e: e.tensor_sub(
                             out=t1s[ci][:, 0:HT],
                             in0=xn1h[:, HT * c:HT * (c + 1)],
                             in1=PB[0][:, 0:HT]))),
                         reads=["xn1h", "PB0"], writes=[f"t1s{ci}"])
                    P.op("vector",
                         (lambda ci=ci, c=c, c0=c0: (lambda e:
                          e.scalar_tensor_tensor(
                              out=xn[:, TT * c + c0:TT * c + c0 + HT],
                              in0=t1s[ci][:, 0:HT],
                              scalar=bcol[:, GCOL0 + c:GCOL0 + c + 1],
                              in1=PB[1][:, 0:HT], op0=ALU.mult, op1=ALU.mult))),
                         reads=[f"t1s{ci}", "PB1", "c_bcol"],
                         writes=["xn"] if c == 0 else [])
                P.res["xn"] = (((("sDVE", P.vals["sDVE"], "vector", False)), []))
                P.op("scalar",
                     (lambda t=t, c0=c0: (lambda e: e.dma_start(
                         out=xn2scr[t][:, :, c0:c0 + HT].rearrange(
                             "c p u -> p c u"),
                         in_=xn[:].rearrange("p (c u) -> p c u", c=DC)[
                             :, :, c0:c0 + HT]))),
                     reads=["xn"], writes=["xn2scr"], dma=True)
                def kvmm(e, c0=c0):
                    for c in range(DC):
                        e.matmul(out=PA[0][:, 0:HT],
                                 lhsT=w13[0][:, 128 * c:128 * (c + 1)],
                                 rhs=xn[:, TT * c + c0:TT * c + c0 + HT],
                                 start=(c == 0), stop=(c == DC - 1))
                    for c in range(DC):
                        mm = e.matmul(out=PA[0][:, HT:2 * HT],
                                      lhsT=w13[0][:, D + 128 * c:
                                                  D + 128 * (c + 1)],
                                      rhs=xn[:, TT * c + c0:TT * c + c0 + HT],
                                      start=(c == 0), stop=(c == DC - 1))
                    return mm
                P.op("tensor", kvmm, reads=["w13b0", "xn"], writes=["PA0"])
                P.op("vector", lambda e: e.tensor_copy(
                    out=krot[0:64, 0:HT], in_=PA[0][64:128, 0:HT]),
                    reads=["PA0"], writes=["krot"])
                P.op("vector", lambda e: e.tensor_copy(
                    out=krot[64:128, 0:HT], in_=PA[0][0:64, 0:HT]),
                    reads=["PA0"], writes=["krot"])
                cs = cost[:, TT * t + c0:TT * t + c0 + HT]
                sn = sint[:, TT * t + c0:TT * t + c0 + HT]
                P.op("vector", (lambda cs=cs: (lambda e: e.tensor_mul(
                    out=t1s[0][:, 0:HT], in0=PA[0][:, 0:HT], in1=cs))),
                    reads=["PA0", "c_tab"], writes=["t1s0"])
                P.op("vector", (lambda sn=sn: (lambda e: e.tensor_mul(
                    out=t1s[1][:, 0:HT], in0=krot[:, 0:HT], in1=sn))),
                    reads=["krot", "c_tab"], writes=["t1s1"])
                P.op("vector", lambda e: e.tensor_add(
                    out=kvb[:, 0:HT], in0=t1s[0][:, 0:HT], in1=t1s[1][:, 0:HT]),
                    reads=["t1s0", "t1s1"], writes=["kvb"])
                P.op("scalar",
                     (lambda t=t, c0=c0: (lambda e: e.dma_start(
                         out=kcon[:, TT * t + c0:TT * t + c0 + HT],
                         in_=kvb[:, 0:HT]))),
                     reads=["kvb"], writes=["kcon"], dma=True)
                P.op("vector", lambda e: e.tensor_copy(
                    out=vtb[:, 0:HT], in_=PA[0][:, HT:2 * HT]),
                    reads=["PA0"], writes=["vtb"])
                for j in range(2):
                    P.op("tensor",
                         (lambda j=j: (lambda e: e.transpose(
                             out=PT[0][:, 128 * j:128 * (j + 1)],
                             in_=vtb[:, 128 * j:128 * (j + 1)],
                             identity=identb))),
                         reads=["vtb", "c_identb"],
                         writes=["PT0"] if j == 0 else [])
                P.res["PT0"] = (((("sPE", P.vals["sPE"], "tensor", False)), []))
                P.op("vector", lambda e: e.tensor_copy(
                    out=kvb[:, 0:HT], in_=PT[0][:, 0:HT]),
                    reads=["PT0"], writes=["kvb"])
                P.op("scalar",
                     (lambda t=t, c0=c0: (lambda e: e.dma_start(
                         out=vcon[TT * t + c0:TT * t + c0 + HT, :].rearrange(
                             "(j p) i -> p j i", p=128),
                         in_=kvb[:, 0:HT].rearrange("p (j i) -> p j i", j=2)))),
                     reads=["kvb"], writes=["vcon"], dma=True)

        P.op("gpsimd", lambda e: e.collective_compute(
            "AllGather", ALU.bypass,
            replica_groups=[[0, 1], [2, 3], [4, 5], [6, 7]],
            ins=[kcon], outs=[kg]),
            reads=["kcon"], writes=["kg"], coll=True)
        P.op("gpsimd", lambda e: e.collective_compute(
            "AllGather", ALU.bypass,
            replica_groups=[[0, 1], [2, 3], [4, 5], [6, 7]],
            ins=[vcon], outs=[vg]),
            reads=["vcon"], writes=["vg"], coll=True)
        for par2 in range(2):
            P.op("scalar",
                 (lambda par2=par2: (lambda e: e.dma_start(
                     out=kseq[:].rearrange("p (g two j) -> p g two j",
                                           two=2, j=128)[:, :, par2, :],
                     in_=kg[128 * par2:128 * (par2 + 1), :].rearrange(
                         "p (g j) -> p g j", j=128)))),
                 reads=["kg"], writes=["kseq"], dma=True)
            P.op("scalar",
                 (lambda par2=par2: (lambda e: e.dma_start(
                     out=vtok[:].rearrange("p (g two j) -> p g two j",
                                           two=2, j=128)[:, :, par2, :],
                     in_=vg[2048 * par2:2048 * (par2 + 1), :].rearrange(
                         "(g p) j -> p g j", p=128)))),
                 reads=["vg"], writes=["vtok"], dma=True)

    def attn_pass():
        for t in range(NT):
            P.op("scalar",
                 (lambda t=t: (lambda e: e.dma_start(
                     out=xn[:], in_=xn2scr[t].rearrange("c p u -> p c u")))),
                 reads=["xn2scr"], writes=["xn"], dma=True)
            for hh in range(2):
                c0 = HT * hh
                for m in range(HEADS):
                    par = m % 2
                    sl = m % 4
                    load_slab("wq", m, 0, f"w13b{sl}", w13[sl], 0)
                    def qmm(e, par=par, sl=sl, c0=c0):
                        for c in range(DC):
                            mm = e.matmul(
                                out=PA[par][:, 0:HT],
                                lhsT=w13[sl][:, 128 * c:128 * (c + 1)],
                                rhs=xn[:, TT * c + c0:TT * c + c0 + HT],
                                start=(c == 0), stop=(c == DC - 1))
                        return mm
                    P.op("tensor", qmm, reads=[f"w13b{sl}", "xn"],
                         writes=[rPA[par]])
                    P.op("vector", (lambda par=par: (lambda e: e.tensor_copy(
                        out=krot[0:64, 0:HT], in_=PA[par][64:128, 0:HT]))),
                        reads=[rPA[par]], writes=["krot"])
                    P.op("vector", (lambda par=par: (lambda e: e.tensor_copy(
                        out=krot[64:128, 0:HT], in_=PA[par][0:64, 0:HT]))),
                        reads=[rPA[par]], writes=["krot"])
                    cs = cost[:, TT * t + c0:TT * t + c0 + HT]
                    sn = sint[:, TT * t + c0:TT * t + c0 + HT]
                    P.op("vector", (lambda par=par, cs=cs: (lambda e:
                        e.tensor_mul(out=t1s[0][:, 0:HT],
                                     in0=PA[par][:, 0:HT], in1=cs))),
                        reads=[rPA[par], "c_tab"], writes=["t1s0"])
                    P.op("vector", (lambda sn=sn: (lambda e: e.tensor_mul(
                        out=t1s[1][:, 0:HT], in0=krot[:, 0:HT], in1=sn))),
                        reads=["krot", "c_tab"], writes=["t1s1"])
                    P.op("vector", (lambda m=m: (lambda e: e.tensor_add(
                        out=qbuf[:, HT * m:HT * (m + 1)], in0=t1s[0][:, 0:HT],
                        in1=t1s[1][:, 0:HT]))),
                        reads=["t1s0", "t1s1"],
                        writes=["qbuf"] if m == 0 else [])
                P.res["qbuf"] = (((("sDVE", P.vals["sDVE"], "vector", False)), []))
                for qb in range(2):
                    lq = 4 * t + 2 * hh + qb
                    L = 256 * (lq + 1)
                    NJ = L // 128
                    KC = (L + 511) // 512
                    qoff = 128 * qb
                    for hg in range(8):
                        hA, hB = 2 * hg, 2 * hg + 1
                        gi = ((t * 2 + hh) * 2 + qb) * 8 + hg
                        opar = gi % 2
                        Sb = [S0, S1]
                        Pb = [P0, P1]
                        Sr = ["S0b", "S1b"]
                        Pr = ["P0b", "P1b"]
                        for hs, h in enumerate((hA, hB)):
                            for kc in range(KC):
                                nn = min(512, L - 512 * kc)
                                def smm(e, h=h, kc=kc, nn=nn, hs=hs, qoff=qoff):
                                    return e.matmul(
                                        out=PA[hs][:, 0:nn],
                                        lhsT=qbuf[:, HT * h + qoff:
                                                  HT * h + qoff + 128],
                                        rhs=kseq[:, 512 * kc:512 * kc + nn],
                                        start=True, stop=True)
                                P.op("tensor", smm,
                                     reads=["qbuf", "kseq"], writes=[rPA[hs]])
                                nmask = 256 if kc == KC - 1 else 0
                                ncopy = nn - nmask
                                if ncopy > 0:
                                    P.op("vector",
                                         (lambda hs=hs, kc=kc, nc=ncopy:
                                          (lambda e: e.tensor_copy(
                                              out=Sb[hs][:, 512 * kc:
                                                         512 * kc + nc],
                                              in_=PA[hs][:, 0:nc]))),
                                         reads=[rPA[hs]],
                                         writes=[Sr[hs]] if kc == 0 else [])
                                if nmask:
                                    P.op("vector",
                                         (lambda hs=hs, kc=kc, nn=nn:
                                          (lambda e: e.tensor_add(
                                              out=Sb[hs][:, 512 * kc + nn - 256:
                                                         512 * kc + nn],
                                              in0=PA[hs][:, nn - 256:nn],
                                              in1=mask))),
                                         reads=[rPA[hs], "c_mask"],
                                         writes=[Sr[hs]] if kc == 0 else [])
                            P.res[Sr[hs]] = ((("sDVE", P.vals["sDVE"],
                                               "vector"), []))
                            P.op("vector",
                                 (lambda hs=hs, L=L: (lambda e: e.reduce_max(
                                     out=m_c, in_=Sb[hs][:, 0:L], axis=AX.X))),
                                 reads=[Sr[hs]], writes=["mc"], strict=True)
                            P.op("vector", lambda e: e.tensor_scalar_mul(
                                out=nb_c, in0=m_c, scalar1=-SCALE),
                                reads=["mc"], writes=["nbc"])
                            P.op("scalar",
                                 (lambda hs=hs, L=L: (lambda e: e.activation(
                                     out=Pb[hs][:, 0:L], in_=Sb[hs][:, 0:L],
                                     func=AF.Exp, bias=nb_c, scale=SCALE))),
                                 reads=[Sr[hs], "nbc"], writes=[Pr[hs]])
                        for jp in range(0, NJ, 2):
                            tpar = (jp // 2) % 2
                            njc = min(2, NJ - jp)
                            def ptg(e, jp=jp, njc=njc, tpar=tpar):
                                for jj in range(njc):
                                    for hs2 in range(2):
                                        mm = e.transpose(
                                            out=PT[tpar][:, 256 * jj + 128 * hs2:
                                                         256 * jj + 128 * (hs2 + 1)],
                                            in_=Pb[hs2][:, 128 * (jp + jj):
                                                        128 * (jp + jj + 1)],
                                            identity=identb)
                                return mm
                            P.op("tensor", ptg,
                                 reads=[Pr[0], Pr[1], "c_identb"],
                                 writes=[rPT[tpar]])
                            P.op("scalar",
                                 (lambda tpar=tpar, njc=njc:
                                  (lambda e: e.copy(
                                      out=PTb[tpar][:, 0:256 * njc],
                                      in_=PT[tpar][:, 0:256 * njc]))),
                                 reads=[rPT[tpar]], writes=[f"PTbb{tpar}"])
                            def pvg(e, jp=jp, njc=njc, tpar=tpar, opar=opar,
                                    NJ=NJ):
                                for jj in range(njc):
                                    e.matmul(out=PY[opar][:, 0:256],
                                             lhsT=vtok[:, 128 * (jp + jj):
                                                       128 * (jp + jj + 1)],
                                             rhs=PTb[tpar][:, 256 * jj:
                                                          256 * (jj + 1)],
                                             start=(jp + jj == 0),
                                             stop=(jp + jj == NJ - 1))
                                    mm = e.matmul(
                                        out=PB[opar][0:1, 0:256],
                                        lhsT=onescb,
                                        rhs=PTb[tpar][:, 256 * jj:256 * (jj + 1)],
                                        start=(jp + jj == 0),
                                        stop=(jp + jj == NJ - 1))
                                return mm
                            wl = [rPY[opar], rPB[opar]] if jp == 0 else []
                            P.op("tensor", pvg,
                                 reads=[f"PTbb{tpar}", "vtok", "c_onescb"],
                                 writes=wl)
                            if jp == 0:
                                pass
                        P.res[rPY[opar]] = ((("sPE", P.vals["sPE"],
                                              "tensor"), []))
                        P.res[rPB[opar]] = ((("sPE", P.vals["sPE"],
                                              "tensor"), []))
                        P.op("vector",
                             (lambda opar=opar: (lambda e: e.tensor_copy(
                                 out=den_r, in_=PB[opar][0:1, 0:256]))),
                             reads=[rPB[opar]], writes=["r_den"])
                        P.op("vector", lambda e: e.reciprocal(
                            out=rr_r, in_=den_r), reads=["r_den"],
                            writes=["r_rr"], strict=True)
                        P.op("tensor",
                             (lambda opar=opar: (lambda e: e.matmul(
                                 out=PB[opar][:, 256:512], lhsT=onesrf,
                                 rhs=rr_r, start=True, stop=True))),
                             reads=["r_rr", "c_onesrf"], writes=[rPB[opar]])
                        P.op("vector",
                             (lambda opar=opar: (lambda e: e.tensor_copy(
                                 out=rbc[:, 0:256], in_=PB[opar][:, 256:512]))),
                             reads=[rPB[opar]], writes=["rbcb"])
                        for hs, h in enumerate((hA, hB)):
                            P.op("vector",
                                 (lambda opar=opar, hs=hs, h=h, qoff=qoff:
                                  (lambda e: e.tensor_mul(
                                      out=Ao[:, HT * h + qoff:
                                             HT * h + qoff + 128],
                                      in0=PY[opar][:, 128 * hs:128 * (hs + 1)],
                                      in1=rbc[:, 128 * hs:128 * (hs + 1)]))),
                                 reads=[rPY[opar], "rbcb"],
                                 writes=["Aob"] if (qb == 0 and hg == 0
                                                    and hs == 0) else [])
                P.res["Aob"] = (((("sDVE", P.vals["sDVE"], "vector", False)), []))
                for m in range(DC):
                    par = m % 2
                    sl = m % 2
                    load_slab("wo", m, 0, f"w2b{sl}", w2b[sl], 0)
                    def womm(e, par=par, sl=sl):
                        for c in range(DC):
                            mm = e.matmul(out=PA[par][:, 0:HT],
                                          lhsT=w2b[sl][:, 128 * c:128 * (c + 1)],
                                          rhs=Ao[:, HT * c:HT * (c + 1)],
                                          start=(c == 0), stop=(c == DC - 1))
                        return mm
                    P.op("tensor", womm, reads=[f"w2b{sl}", "Aob"],
                         writes=[rPA[par]])
                    P.op("scalar",
                         (lambda t=t, m=m, par=par, c0=c0:
                          (lambda e: e.dma_start(
                              out=xs[par][:, 0:HT],
                              in_=xmid[t, m][:, c0:c0 + HT]))),
                         reads=["xmid"], writes=[f"xs{par}"], dma=True)
                    P.op("vector",
                         (lambda par=par: (lambda e: e.tensor_add(
                             out=xs[par][:, 0:HT], in0=xs[par][:, 0:HT],
                             in1=PA[par][:, 0:HT]))),
                         reads=[rPA[par]], writes=[f"xs{par}"])
                    P.op("scalar",
                         (lambda t=t, m=m, par=par, c0=c0:
                          (lambda e: e.dma_start(
                              out=xmid[t, m][:, c0:c0 + HT],
                              in_=xs[par][:, 0:HT]))),
                         reads=[f"xs{par}"], writes=["xmid"], dma=True)

    consts = [(bcol, bcols_d, "c_bcol"), (identb, identb_d, "c_identb"),
              (identf, identf_d, "c_identf"), (identh, identh_d, "c_identh"),
              (onescb, onescb_d, "c_onescb"), (onesrf, onesrf_d, "c_onesrf"),
              (mask, mask_d, "c_mask"), (cost, cost_d, "c_tab"),
              (sint, sint_d, "c_tab"), (b2aug[0:17, :], b2aug_d, "c_b2aug"),
              (sel, sel_d, "c_sel"), (gbb, gbb_d, "c_gbb"),
              (gatep, gatep_d, "c_gatep")]
    def load_consts():
        for (dst, src, rn) in consts:
            P.op("gpsimd",
                 (lambda d=dst, s=src: (lambda e: e.dma_start(out=d, in_=s))),
                 writes=[rn], dma=True)
        P.op("vector", lambda e: e.memset(cwaug[:], 0.0), writes=["cwaug"])
        P.op("gpsimd", lambda e: e.dma_start(out=cwaug[16:17, :], in_=halfrow_d),
             reads=["cwaug"], writes=["cwaug"], dma=True)

    load_consts()
    ffn_phase(1, True, False)
    if MK_STOP != "f1":
        P.barrier()
        kv_pass()
        P.barrier()
        attn_pass()
        if MK_STOP == "full":
            P.barrier()
            load_consts()
            ffn_phase(2, False, True)

    if MK_STOP == "f1":
        src_res = "xmid"
    elif MK_STOP == "attn":
        src_res = "xmid"
    else:
        src_res = None
    if src_res is not None:
        P.barrier()
        for t in range(NT):
            for m in range(DC):
                P.op("gpsimd",
                     (lambda t=t, m=m: (lambda e: e.dma_start(
                         out=out_d[128 * m:128 * (m + 1), TT * t:TT * (t + 1)],
                         in_=xmid[t, m]))),
                     reads=["xmid"], writes=["out"], dma=True)

    w, rd = P.res.get("out", (None, []))
    P.op("gpsimd", lambda e: e.nop(), extra=[w] if w else [])
    P.build()
    return nc



def _host_pack(inputs):
    f32 = np.float32
    x = np.asarray(inputs["L0_attn_x"], f32)
    gamma = np.asarray(inputs["L0_attn_gamma"], f32)
    Wq = np.asarray(inputs["L0_attn_Wq"], f32)
    Wkv = np.asarray(inputs["L0_attn_Wkv"], f32)
    Wo = np.asarray(inputs["L0_attn_Wo"], f32)
    gate_w = np.asarray(inputs["L0_ffn_gate_w"], f32)
    gate_b = np.asarray(inputs["L0_ffn_gate_b"], f32)
    w1 = np.asarray(inputs["L0_ffn_w1"], f32)
    b1 = np.asarray(inputs["L0_ffn_b1"], f32)
    w2 = np.asarray(inputs["L0_ffn_w2"], f32)
    b2 = np.asarray(inputs["L0_ffn_b2"], f32)
    w3 = np.asarray(inputs["L0_ffn_w3"], f32)
    b3 = np.asarray(inputs["L0_ffn_b3"], f32)
    ws1 = np.asarray(inputs["L0_ffn_ws1"], f32)
    bs1 = np.asarray(inputs["L0_ffn_bs1"], f32)
    ws2 = np.asarray(inputs["L0_ffn_ws2"], f32)
    bs2 = np.asarray(inputs["L0_ffn_bs2"], f32)
    ws3 = np.asarray(inputs["L0_ffn_ws3"], f32)
    bs3 = np.asarray(inputs["L0_ffn_bs3"], f32)

    C = np.ascontiguousarray
    shared = dict(
        w1t=C(w1.transpose(0, 2, 1)), w3t=C(w3.transpose(0, 2, 1)),
        w2t=C(w2.transpose(0, 2, 1)),
        ws1t=C(ws1.T), ws3t=C(ws3.T), ws2t=C(ws2.T),
        wq=C(Wq), wkv=C(Wkv), wo=C(Wo),
    )
    bcols = np.zeros((128, BCOLS), f32)
    for e in range(E):
        for k in range(HC):
            bcols[:, b1col(e, k)] = b1[e, 128 * k:128 * (k + 1)]
            bcols[:, b3col(e, k)] = b3[e, 128 * k:128 * (k + 1)]
    for eh in range(2):
        for k in range(HC):
            bcols[:, b1col(E + eh, k)] = bs1[1408 * eh + 128 * k:
                                             1408 * eh + 128 * (k + 1)]
            bcols[:, b3col(E + eh, k)] = bs3[1408 * eh + 128 * k:
                                             1408 * eh + 128 * (k + 1)]
    for c in range(DC):
        bcols[:, GCOL0 + c] = gamma[128 * c:128 * (c + 1)]
    shared["bcols"] = bcols
    shared["b2aug"] = C(np.concatenate([b2, bs2[None, :]], 0))
    gp = gate_w.T.reshape(DC, 128, E).transpose(1, 0, 2).reshape(128, DC * E)
    shared["gatep"] = C(gp)
    shared["gbb"] = C(np.broadcast_to(gate_b, (128, E)))
    sel = np.zeros((32, NEXP * 128), f32)
    for e in range(NEXP):
        sel[min(e, 16), 128 * e:128 * (e + 1)] = 1.0
    shared["sel"] = sel
    shared["halfrow"] = np.full((1, TT), 0.5, f32)
    shared["identb"] = np.eye(128, dtype=ml_dtypes.bfloat16)
    shared["identf"] = np.eye(128, dtype=f32)
    shared["identh"] = (0.5 * np.eye(128)).astype(f32)
    shared["onescb"] = np.ones((128, 1), ml_dtypes.bfloat16)
    shared["onesrf"] = np.ones((1, 128), f32)

    tri = np.zeros((128, 128), f32)
    tri[np.triu_indices(128, 1)] = NEG
    full = np.full((128, 128), NEG, f32)
    zero = np.zeros((128, 128), f32)
    masks = {0: np.concatenate([tri, full], 1),
             1: np.concatenate([zero, tri], 1)}

    inv = (10000.0 ** (-(np.arange(0, DH, 2, dtype=f32) / DH))).astype(f32)
    tabs = {}
    for hlf in range(2):
        pos = local_positions(hlf).astype(f32)
        fr = pos[None, :] * inv[:, None]
        fr = np.concatenate([fr, fr], 0)
        ct = np.cos(fr).astype(f32)
        st = np.sin(fr).astype(f32)
        st[0:64, :] *= -1.0
        tabs[hlf] = (C(ct), C(st))

    percore = []
    for c in range(8):
        s, hlf = c // 2, c % 2
        idx = local_positions(hlf)
        d = dict(shared)
        d["x_in"] = C(x[s][idx].T)
        d["costab"], d["sintab"] = tabs[hlf]
        d["mask"] = masks[hlf]
        percore.append(d)
    return percore


_NC_CACHE = {}


def kernel(**inputs):
    percore = _host_pack(inputs)
    if 0 not in _NC_CACHE:
        _NC_CACHE[0] = build_nc()
    nc = _NC_CACHE[0]
    res = run_bass_kernel_spmd(nc, percore, core_ids=list(range(8)))
    out = np.empty((B, N, D), np.float32)
    for c in range(8):
        s, hlf = c // 2, c % 2
        idx = local_positions(hlf)
        out[s][idx] = res.results[c]["out"].T
    return out
